# version 06
# baseline (speedup 1.0000x reference)
import numpy as np
import contextlib
import concourse.bass as bass
import concourse.mybir as mybir
from concourse.bass_utils import run_bass_kernel_spmd

F32 = mybir.dt.float32
F32R = mybir.dt.float32r
AF = mybir.ActivationFunctionType
ALU = mybir.AluOpType
AX = mybir.AxisListType

MM_DT = F32

B, T, E = 4, 4096, 2048
KV, G, D = 8, 4, 64
NOT_ = 16
TOK = NOT_ * 128

NH, P, N, CS = 64, 64, 128, 256
DIN = NH * P
CONVD = DIN + 2 * N
NHh = NH // 2
ZH = NHh * P
PRJL = ZH + ZH + 2 * N + NHh
XBCL = ZH + 2 * N

NE, RK, HID = 8, 16, 5632
ALPHA = 2.0


def host_prep(ins, core):
    b, par = core // 2, core % 2
    x = np.asarray(ins["L0_attn_x"][b])
    own_rows = np.concatenate([np.arange(128 * (2 * j + par), 128 * (2 * j + par) + 128)
                               for j in range(NOT_)])
    m = {}
    m["x_full"] = np.ascontiguousarray(x)
    m["x_own"] = np.ascontiguousarray(x[own_rows])

    WqT = ins["L0_attn_Wq"].T
    m["WqTB"] = np.ascontiguousarray(
        WqT.reshape(16, 128, 16, 128).transpose(2, 1, 0, 3).reshape(16, 128, 2048))
    m["WkT"] = np.ascontiguousarray(ins["L0_attn_Wk"].T)
    m["WvT"] = np.ascontiguousarray(ins["L0_attn_Wv"].T)
    WoT = ins["L0_attn_Wo"].T
    m["WoTB"] = np.ascontiguousarray(
        WoT.reshape(16, 128, 4, 512).transpose(2, 1, 0, 3).reshape(4, 128, 8192))
    bq = np.asarray(ins["L0_attn_bq"])
    m["bq"] = np.ascontiguousarray(bq.reshape(16, 128).T)
    idx = np.arange(E)
    shp = (idx // 64) * 64 + ((idx % 64) + 32) % 64
    m["bqs"] = np.ascontiguousarray(bq[shp].reshape(16, 128).T)
    bk = np.asarray(ins["L0_attn_bk"])
    m["bk"] = np.ascontiguousarray(bk.reshape(4, 128).T)
    i5 = np.arange(512)
    shk = (i5 // 64) * 64 + ((i5 % 64) + 32) % 64
    m["bks"] = np.ascontiguousarray(bk[shk].reshape(4, 128).T)
    m["bv"] = np.ascontiguousarray(np.asarray(ins["L0_attn_bv"]).reshape(1, 512))
    m["bo"] = np.ascontiguousarray(np.asarray(ins["L0_attn_bo"]).reshape(1, 2048))

    j32 = np.arange(32, dtype=np.float64)
    theta = 10000.0 ** (-2.0 * j32 / 64.0)
    posf = np.arange(T, dtype=np.float64) + 1.0
    ang = posf[None, :] * np.concatenate([theta, theta])[:, None]
    cos64 = np.cos(ang)
    sin64 = np.sin(ang)
    sineff = np.concatenate([-sin64[:32], sin64[32:]], axis=0)
    cosk = np.tile(cos64, (2, 1)).astype(np.float32)
    sink = np.tile(sineff, (2, 1)).astype(np.float32)
    m["cosk"] = np.ascontiguousarray(cosk)
    m["sink"] = np.ascontiguousarray(sink)
    m["cosq"] = np.ascontiguousarray(cosk[:, own_rows])
    m["sinq"] = np.ascontiguousarray(sink[:, own_rows])

    msk = np.zeros((8, 128, 512), np.float32)
    tril_strict = np.tril(np.ones((128, 128), np.float32), -1)
    for r in range(8):
        for u in range(4):
            qb = par + 2 * u
            if r == qb:
                msk[r][:, 128 * u:128 * u + 128] += tril_strict * (-1e30)
            elif r > qb:
                msk[r][:, 128 * u:128 * u + 128] += -1e30
    m["amask"] = np.ascontiguousarray(msk.transpose(1, 0, 2).reshape(128, 8 * 512))
    m["eye"] = np.eye(128, dtype=np.float32)

    m["WfT"] = np.ascontiguousarray(ins["L0_ffn_weight"].T)
    m["gamma"] = np.ascontiguousarray(np.asarray(ins["L0_ffn_norm_weight"]).reshape(1, 2048))

    h0 = par * NHh
    W_in = np.asarray(ins["L1_attn_W_in"])
    Wl = np.concatenate([W_in[h0 * P:h0 * P + ZH],
                         W_in[DIN + h0 * P:DIN + h0 * P + ZH],
                         W_in[2 * DIN:2 * DIN + 2 * N],
                         W_in[DIN + CONVD + h0:DIN + CONVD + h0 + NHh]], axis=0)
    WinT = Wl.T
    m["WinTB"] = np.ascontiguousarray(
        np.pad(WinT, ((0, 0), (0, 4480 - PRJL))).reshape(16, 128, 35, 128)
        .transpose(2, 1, 0, 3).reshape(35, 128, 2048))
    cw = np.asarray(ins["L1_attn_conv_w"])[:, 0, :]
    cwl = np.concatenate([cw[h0 * P:h0 * P + ZH], cw[DIN:]], axis=0)
    m["convw"] = np.ascontiguousarray(cwl.reshape(18, 128, 4).transpose(1, 0, 2).reshape(128, 72))
    cb = np.asarray(ins["L1_attn_conv_b"])
    cbl = np.concatenate([cb[h0 * P:h0 * P + ZH], cb[DIN:]], axis=0)
    m["convb"] = np.ascontiguousarray(cbl.reshape(18, 128).T)
    m["dtb"] = np.ascontiguousarray(np.asarray(ins["L1_attn_dt_bias"][h0:h0 + NHh]).reshape(NHh, 1))
    m["Alog"] = np.ascontiguousarray(np.asarray(ins["L1_attn_A_log"][h0:h0 + NHh]).reshape(NHh, 1))
    m["Drep"] = np.ascontiguousarray(
        np.repeat(np.asarray(ins["L1_attn_D_param"][h0:h0 + NHh]), P).reshape(32, 64).T)
    m["nw"] = np.ascontiguousarray(
        np.asarray(ins["L1_attn_norm_w"][h0 * P:h0 * P + ZH]).reshape(16, 128).T)
    WoutT = np.asarray(ins["L1_attn_W_out"])[:, h0 * P:h0 * P + ZH].T
    m["WoutTB"] = np.ascontiguousarray(
        WoutT.reshape(16, 128, 4, 512).transpose(2, 1, 0, 3).reshape(4, 128, 8192))
    pat = np.zeros((2, 128, 256), np.float32)
    for po in range(2):
        s = np.arange(128)[:, None] + 128 * po
        l = np.arange(256)[None, :]
        pat[po] = (s <= l).astype(np.float32)
    m["triu"] = np.ascontiguousarray(pat.transpose(1, 0, 2).reshape(128, 512))

    WupT = np.asarray(ins["L1_ffn_w_up"]).T
    WgT = np.asarray(ins["L1_ffn_w_gate"]).T
    m["WupTB"] = np.ascontiguousarray(
        WupT.reshape(16, 128, 44, 128).transpose(2, 1, 0, 3).reshape(44, 128, 2048))
    m["WgTB"] = np.ascontiguousarray(
        WgT.reshape(16, 128, 44, 128).transpose(2, 1, 0, 3).reshape(44, 128, 2048))
    WdT = np.asarray(ins["L1_ffn_w_down"]).T
    m["WdTB"] = np.ascontiguousarray(
        WdT.reshape(44, 128, 16, 128).transpose(2, 1, 0, 3).reshape(16, 128, 5632))
    m["gwTB"] = np.ascontiguousarray(
        np.asarray(ins["L1_ffn_gate_w"]).T.reshape(16, 128, 8).transpose(1, 0, 2)
        .reshape(128, 128))
    m["upAB"] = np.ascontiguousarray(
        np.transpose(ins["L1_ffn_up_A"], (0, 2, 1)).reshape(8, 16, 128, 16)
        .transpose(0, 2, 1, 3).reshape(8, 128, 256))
    m["gaAB"] = np.ascontiguousarray(
        np.transpose(ins["L1_ffn_gate_A"], (0, 2, 1)).reshape(8, 16, 128, 16)
        .transpose(0, 2, 1, 3).reshape(8, 128, 256))
    upBT = np.transpose(ins["L1_ffn_up_B"], (0, 2, 1))
    gaBT = np.transpose(ins["L1_ffn_gate_B"], (0, 2, 1))
    m["upBB"] = np.ascontiguousarray(
        upBT.reshape(8, 16, 44, 128).transpose(2, 1, 0, 3).reshape(44, 16, 1024))
    m["gaBB"] = np.ascontiguousarray(
        gaBT.reshape(8, 16, 44, 128).transpose(2, 1, 0, 3).reshape(44, 16, 1024))
    dnAT = np.transpose(ins["L1_ffn_down_A"], (0, 2, 1))
    m["dnAB"] = np.ascontiguousarray(
        dnAT.reshape(8, 44, 128, 16).transpose(1, 2, 0, 3).reshape(44, 128, 128))
    m["dnBT"] = np.ascontiguousarray(np.transpose(ins["L1_ffn_down_B"], (0, 2, 1)))
    return m


class Chain:
    def __init__(self):
        self.ops = []
        self.solo = False

    def add(self, eng, fn, w=1):
        self.ops.append((eng, fn, w, self.solo))

    def dma(self, eng, **kw):
        self.add(eng, lambda e, kw=kw: e.dma_start(**kw), 16)

    def coll(self, kind, op, groups, ins, outs):
        self.add("gpsimd", lambda e: e.collective_compute(
            kind, op, replica_groups=groups, ins=ins, outs=outs), 1)

    def emit(self, nc, block, ch):
        n = len(self.ops)
        waits = []
        tot = 0
        for op in self.ops:
            waits.append(tot)
            tot += op[2]
        final = tot
        runs = []
        i = 0
        while i < n:
            eng = self.ops[i][0]
            isdma = self.ops[i][2] == 16
            j = i
            while j < n and self.ops[j][0] == eng and (self.ops[j][2] == 16) == isdma:
                j += 1
            runs.append((i, j, waits[i], isdma))
            i = j
        per_eng = {}
        for r in runs:
            per_eng.setdefault(self.ops[r[0]][0], []).append(r)

        def runner(eng_name):
            def body(e):
                for (i0, j0, wv, isdma) in per_eng.get(eng_name, []):
                    if isdma:
                        e.wait_ge(ch, wv)
                        for k in range(i0, j0):
                            self.ops[k][1](e).then_inc(ch, 16)
                    elif eng_name in ("tensor", "scalar"):
                        run_w = wv
                        k = i0
                        while k < j0:
                            if eng_name == "scalar" and self.ops[k][3]:
                                e.wait_ge(ch, run_w)
                                self.ops[k][1](e).then_inc(ch, self.ops[k][2])
                                run_w += self.ops[k][2]
                                k += 1
                                continue
                            k2 = k
                            cnt = 0
                            while (k2 < j0 and not
                                   (eng_name == "scalar" and self.ops[k2][3])):
                                cnt += self.ops[k2][2]
                                k2 += 1
                            e.wait_ge(ch, run_w)
                            last = None
                            for kk in range(k, k2):
                                last = self.ops[kk][1](e)
                            last.then_inc(ch, cnt)
                            run_w += cnt
                            k = k2
                    else:
                        run_w = wv
                        for k in range(i0, j0):
                            e.wait_ge(ch, run_w)
                            inst = self.ops[k][1](e)
                            if inst is None:
                                inst = e.nop()
                            inst.then_inc(ch, self.ops[k][2])
                            run_w += self.ops[k][2]
                if eng_name == "gpsimd":
                    e.wait_ge(ch, final)
            return body

        block.sync(runner("sync"))
        block.tensor(runner("tensor"))
        block.vector(runner("vector"))
        block.scalar(runner("scalar"))
        block.gpsimd(runner("gpsimd"))


class Arena:
    def __init__(self, sb):
        self.sb = sb
        self.off = 0
        self.marks = []

    def al(self, cols, parts=128):
        o = (self.off + 7) & ~7
        self.off = o + cols
        assert self.off <= self.sb.shape[1], f"SBUF arena overflow: {self.off}"
        return self.sb[0:parts, o:o + cols]

    def at(self, off, cols, parts=128):
        assert off + cols <= self.sb.shape[1]
        return self.sb[0:parts, off:off + cols]

    def push(self):
        self.marks.append(self.off)

    def pop(self):
        self.off = self.marks.pop()


def build(stages=("A", "B", "C", "D"), dbg=()):
    nc = bass.Bass("TRN2", target_bir_lowering=False)

    def din(name, shape, dt=F32):
        return nc.dram_tensor(name, shape, dt, kind="ExternalInput").ap()

    def dint(name, shape, dt=F32):
        return nc.dram_tensor(name, shape, dt, kind="Internal").ap()

    def dout(name, shape, dt=F32):
        return nc.dram_tensor(name, shape, dt, kind="ExternalOutput").ap()

    I = {}
    I["x_full"] = din("x_full", [T, E])
    I["x_own"] = din("x_own", [TOK, E])
    for nm, sh in [("WqTB", [16, 128, 2048]), ("WkT", [E, 512]), ("WvT", [E, 512]),
                   ("WoTB", [4, 128, 8192]), ("bv", [1, 512]), ("bo", [1, 2048])]:
        I[nm] = din(nm, sh, MM_DT)
    for nm, sh in [("bq", [128, 16]), ("bqs", [128, 16]), ("bk", [128, 4]),
                   ("bks", [128, 4]),
                   ("cosk", [128, T]), ("sink", [128, T]),
                   ("cosq", [128, TOK]), ("sinq", [128, TOK]),
                   ("amask", [128, 8 * 512]), ("eye", [128, 128]),
                   ("gamma", [1, 2048])]:
        I[nm] = din(nm, sh)
    I["WfT"] = din("WfT", [E, E])

    scr = {}
    scr["QT"] = dint("QT", [E, TOK], MM_DT)
    scr["KT"] = dint("KT", [512, T], MM_DT)
    scr["KTO"] = dint("KTO", [512, TOK], MM_DT)
    scr["dbgST"] = dint("dbgST", [128, 8])
    scr["dbgR"] = dint("dbgR", [128, 2048])
    scr["V8"] = dint("V8", [KV, 128, 2048], MM_DT)
    scr["yTB"] = dint("yTB", [NOT_, 128, 2048], MM_DT)
    scr["oD"] = dint("oD", [TOK, E])
    scr["x1D"] = dint("x1D", [TOK, E])
    scr["x2D"] = dint("x2D", [TOK, E])
    scr["x3D"] = dint("x3D", [TOK, E])
    scr["xnD"] = dint("xnD", [TOK, E])
    scr["xqTB"] = dint("xqTB", [NOT_, 128, 2048], MM_DT)
    scr["o2D"] = dint("o2D", [TOK, E])
    out_own = dout("out_own", [TOK, E])
    scr["out_own"] = out_own

    if "B" in stages:
        scr["mx_i"] = dint("mx_i", [1, 8])
        scr["mx_o"] = dint("mx_o", [1, 8])
    if "C" in stages:
        for nm, sh, dt in [("WinTB", [35, E, 128], MM_DT), ("convw", [128, 72], F32),
                           ("convb", [128, 18], F32), ("dtb", [NHh, 1], F32),
                           ("Alog", [NHh, 1], F32), ("Drep", [64, 32], F32),
                           ("nw", [128, 16], F32), ("WoutTB", [4, 128, 8192], MM_DT),
                           ("triu", [128, 512], F32)]:
            I[nm] = din(nm, sh, dt)
        scr["ag_i"] = dint("ag_i", [TOK, E])
        scr["ag_o"] = dint("ag_o", [2, TOK, E])
        scr["x2T"] = dint("x2T", [E, T], MM_DT)
        scr["prjD"] = dint("prjD", [4480, T])
        scr["gD"] = dint("gD", [ZH, T])
        scr["xcD"] = dint("xcD", [XBCL, T], MM_DT)
        scr["yMD"] = dint("yMD", [ZH, T], F32)
        scr["gnB"] = dint("gnB", [32, 128, 2048], MM_DT)
        scr["ss_i"] = dint("ss_i", [1, T])
        scr["ss_o"] = dint("ss_o", [1, T])
        scr["rs_i"] = dint("rs_i", [T, E])
        scr["rs_o"] = dint("rs_o", [TOK, E])
    if "D" in stages:
        for nm, sh in [("WupTB", [44, 128, 2048]), ("WgTB", [44, 128, 2048]),
                       ("WdTB", [16, 128, 5632]), ("gwTB", [128, 128]),
                       ("upAB", [NE, 128, 256]), ("gaAB", [NE, 128, 256]),
                       ("upBB", [44, 16, 1024]), ("gaBB", [44, 16, 1024]),
                       ("dnAB", [44, 128, 128]), ("dnBT", [NE, RK, E])]:
            I[nm] = din(nm, sh, MM_DT)
        scr["tD"] = dint("tD", [32, 16, TOK], MM_DT)
        scr["TaD"] = dint("TaD", [NE, 16, TOK], MM_DT)
        scr["x3T"] = dint("x3T", [E, TOK], MM_DT)
        scr["mixD"] = dint("mixD", [HID, TOK], MM_DT)
        scr["tm_i"] = dint("tm_i", [1, 8])
        scr["tm_o"] = dint("tm_o", [1, 8])
        scr["ts_i"] = dint("ts_i", [1, 8])
        scr["ts_o"] = dint("ts_o", [1, 8])
        scr["crD"] = dint("crD", [NE, 1, TOK])

    with contextlib.ExitStack() as stack:
        en = stack.enter_context
        en(nc.allow_low_precision(reason="float32r matmul operand tiles (4-byte fp32 storage)"))
        sbF = en(nc.sbuf_tensor([128, 21504], F32))
        sbR = en(nc.sbuf_tensor([128, 28672], MM_DT))
        ar = Arena(sbF)
        arR = Arena(sbR)
        idxu = en(nc.sbuf_tensor([128, 16], mybir.dt.uint32))
        psA = en(nc.psum_tensor([128, 2048], F32))
        psB = en(nc.psum_tensor([128, 2048], F32))
        ch = en(nc.semaphore("ch"))
        block = en(nc.Block())
        c = Chain()

        eyesb = ar.al(128)
        c.dma("sync", out=eyesb, in_=I["eye"])
        onesrF = ar.al(128, parts=1)
        c.add("vector", lambda e: e.memset(onesrF, 1.0))
        onescF = ar.al(8)
        c.add("vector", lambda e: e.memset(onescF[:, 0:1], 1.0))
        onesr = arR.al(128, parts=1)
        c.add("scalar", lambda e: e.activation(onesr, onesrF, AF.Copy, bias=0.0, scale=1.0))
        onesc = arR.al(8)
        c.add("scalar", lambda e: e.activation(onesc[:, 0:1], onescF[:, 0:1], AF.Copy,
                                               bias=0.0, scale=1.0))
        epscol = ar.al(8)
        c.add("vector", lambda e: e.memset(epscol[:, 0:1], 1e-5))

        ctx = dict(I=I, scr=scr, ar=ar, arR=arR, c=c, psA=psA, psB=psB, eyesb=eyesb,
                   onesr=onesr, onesc=onesc, onesrF=onesrF, onescF=onescF,
                   epscol=epscol, nc=nc, idxu=idxu)

        if "A" in stages:
            stage_A(ctx)
        if "B" in stages:
            stage_B(ctx)
        if "C" in stages:
            stage_C(ctx)
        if "D" in stages:
            stage_D(ctx)

        import os as _os
        _tr = _os.environ.get("K_TRUNC")
        if _tr:
            c.ops = c.ops[:int(_tr)]
        for name in dbg:
            src = scr[name]
            o = dout("dbg_" + name, list(src.shape))
            c.dma("gpsimd", out=o, in_=src)

        c.emit(nc, block, ch)
    return nc


def stage_A(ctx):
    I, scr, ar, arR, c = ctx["I"], ctx["scr"], ctx["ar"], ctx["arR"], ctx["c"]
    _scr = scr
    psA, psB = ctx["psA"], ctx["psB"]
    eyesb, onesr, onesc, epscol = ctx["eyesb"], ctx["onesr"], ctx["onesc"], ctx["epscol"]
    onesrF = ctx["onesrF"]
    onescF = ctx["onescF"]
    QT, KT, V8, yTB = scr["QT"], scr["KT"], scr["V8"], scr["yTB"]
    KTO = scr["KTO"]
    oD, x1D = scr["oD"], scr["x1D"]

    def rope_evac(ps, bcol, bscol, ctab, stab, out, ev, shf, t2):
        c.add("vector", lambda e: e.tensor_scalar(ev, ps, bcol, None, ALU.add))
        c.add("vector", lambda e: e.tensor_tensor(ev, ev, ctab, ALU.mult))
        for g0 in (0, 64):
            c.add("scalar", lambda e, g0=g0: e.activation(
                shf[g0:g0 + 32, :], ps[g0 + 32:g0 + 64, :], AF.Copy, bias=0.0, scale=1.0))
            c.add("scalar", lambda e, g0=g0: e.activation(
                shf[g0 + 32:g0 + 64, :], ps[g0:g0 + 32, :], AF.Copy, bias=0.0, scale=1.0))
        c.add("vector", lambda e: e.tensor_scalar(t2, shf, bscol, None, ALU.add))
        c.add("vector", lambda e: e.tensor_tensor(t2, t2, stab, ALU.mult))
        c.add("vector", lambda e: e.tensor_tensor(out, ev, t2, ALU.add))

    ar.push(); arR.push()
    xT = arR.al(16 * 512)
    ko = arR.al(512)
    evR = arR.al(512)
    bvsb = arR.al(512, parts=1)
    WkR = arR.al(16 * 512)
    WvR = arR.al(16 * 512)
    for ec in range(16):
        c.dma("sync", out=WkR[:, 512 * ec:512 * ec + 512],
              in_=I["WkT"][128 * ec:128 * ec + 128, :])
        c.dma("sync", out=WvR[:, 512 * ec:512 * ec + 512],
              in_=I["WvT"][128 * ec:128 * ec + 128, :])
    bksb = ar.al(8)
    bkssb = ar.al(8)
    c.dma("sync", out=bksb[:, 0:4], in_=I["bk"])
    c.dma("sync", out=bkssb[:, 0:4], in_=I["bks"])
    c.dma("sync", out=bvsb, in_=I["bv"])
    ckk = ar.al(4096)
    skk = ar.al(4096)
    c.dma("sync", out=ckk, in_=I["cosk"])
    c.dma("sync", out=skk, in_=I["sink"])
    xtb = ar.al(4 * 2048)
    ev = ar.al(512)
    shf = ar.al(512)
    t2 = ar.al(512)

    for tb in range(8):
        for u in range(4):
            c.dma("sync", out=xtb[:, 2048 * u:2048 * u + 2048],
                  in_=I["x_full"][512 * tb + 128 * u:512 * tb + 128 * u + 128, :])
        for ec in range(16):
            pst = psA if ec % 2 == 0 else psB
            for u in range(4):
                c.add("tensor", lambda e, d=pst[:, 128 * u:128 * u + 128],
                      s=xtb[:, 2048 * u + 128 * ec:2048 * u + 128 * ec + 128],
                      ey=eyesb: e.transpose(d, s, ey))
            c.add("scalar", lambda e, d=xT[:, 512 * ec:512 * ec + 512],
                  s=pst[:, 0:512]: e.activation(d, s, AF.Copy, bias=0.0, scale=1.0))
        for kt in range(4):
            for ec in range(16):
                c.add("tensor", lambda e, d=psA[:, 512 * kt:512 * kt + 512],
                      l=WkR[:, 512 * ec + 128 * kt:512 * ec + 128 * kt + 128],
                      r=xT[:, 512 * ec:512 * ec + 512],
                      st=(ec == 0), sp=(ec == 15): e.matmul(d, l, r, start=st, stop=sp))
        for kt in range(4):
            rope_evac(psA[:, 512 * kt:512 * kt + 512], bksb[:, kt:kt + 1],
                      bkssb[:, kt:kt + 1], ckk[:, 512 * tb:512 * tb + 512],
                      skk[:, 512 * tb:512 * tb + 512], ko, ev, shf, t2)
            c.dma("gpsimd", out=KT[128 * kt:128 * kt + 128, 512 * tb:512 * tb + 512],
                  in_=ko)
        for u in range(4):
            for ec in range(16):
                c.add("tensor", lambda e,
                      l=xT[:, 512 * ec + 128 * u:512 * ec + 128 * u + 128],
                      r=WvR[:, 512 * ec:512 * ec + 512],
                      st=(ec == 0): e.matmul(psB[:, 0:512], l, r, start=st, stop=False))
            c.add("tensor", lambda e: e.matmul(
                psB[:, 0:512], onesr, bvsb, start=False, stop=True))
            c.add("scalar", lambda e: e.activation(
                evR, psB[:, 0:512], AF.Copy, bias=0.0, scale=1.0))
            for h in range(8):
                c.dma("gpsimd",
                      out=V8[h][:, 64 * (4 * tb + u):64 * (4 * tb + u) + 64],
                      in_=evR[:, 64 * h:64 * h + 64])
    ar.pop(); arR.pop()

    ar.push(); arR.push()
    bqsb = ar.al(16)
    bqssb = ar.al(16)
    c.dma("sync", out=bqsb, in_=I["bq"])
    c.dma("sync", out=bqssb, in_=I["bqs"])
    cqq = ar.al(2048)
    sqq = ar.al(2048)
    c.dma("sync", out=cqq, in_=I["cosq"])
    c.dma("sync", out=sqq, in_=I["sinq"])
    bk2 = ar.al(8)
    bks2 = ar.al(8)
    c.dma("sync", out=bk2[:, 0:4], in_=I["bk"])
    c.dma("sync", out=bks2[:, 0:4], in_=I["bks"])
    xtb = ar.al(4 * 2048)
    ev = ar.al(512)
    shf = ar.al(512)
    t2 = ar.al(512)
    xT = arR.al(16 * 512)
    wq = arR.al(16 * 128)
    qo = arR.al(512)
    Wk2 = arR.al(16 * 512)
    for ec in range(16):
        c.dma("sync", out=Wk2[:, 512 * ec:512 * ec + 512],
              in_=I["WkT"][128 * ec:128 * ec + 128, :])
    for cb in range(4):
        for u in range(4):
            c.dma("sync", out=xtb[:, 2048 * u:2048 * u + 2048],
                  in_=I["x_own"][512 * cb + 128 * u:512 * cb + 128 * u + 128, :])
        for ec in range(16):
            pst = psA if ec % 2 == 0 else psB
            for u in range(4):
                c.add("tensor", lambda e, d=pst[:, 128 * u:128 * u + 128],
                      s=xtb[:, 2048 * u + 128 * ec:2048 * u + 128 * ec + 128],
                      ey=eyesb: e.transpose(d, s, ey))
            c.add("scalar", lambda e, d=xT[:, 512 * ec:512 * ec + 512],
                  s=pst[:, 0:512]: e.activation(d, s, AF.Copy, bias=0.0, scale=1.0))
        for qt in range(16):
            c.dma("sync", out=wq, in_=I["WqTB"][qt])
            for ec in range(16):
                c.add("tensor", lambda e, l=wq[:, 128 * ec:128 * ec + 128],
                      r=xT[:, 512 * ec:512 * ec + 512],
                      st=(ec == 0), sp=(ec == 15): e.matmul(psA[:, 0:512], l, r,
                                                           start=st, stop=sp))
            rope_evac(psA[:, 0:512], bqsb[:, qt:qt + 1], bqssb[:, qt:qt + 1],
                      cqq[:, 512 * cb:512 * cb + 512], sqq[:, 512 * cb:512 * cb + 512],
                      qo, ev, shf, t2)
            c.dma("gpsimd", out=QT[128 * qt:128 * qt + 128, 512 * cb:512 * cb + 512],
                  in_=qo)
        for kt in range(4):
            for ec in range(16):
                c.add("tensor", lambda e, d=psA[:, 512 * kt:512 * kt + 512],
                      l=Wk2[:, 512 * ec + 128 * kt:512 * ec + 128 * kt + 128],
                      r=xT[:, 512 * ec:512 * ec + 512],
                      st=(ec == 0), sp=(ec == 15): e.matmul(d, l, r, start=st, stop=sp))
        for kt in range(4):
            rope_evac(psA[:, 512 * kt:512 * kt + 512], bk2[:, kt:kt + 1],
                      bks2[:, kt:kt + 1], cqq[:, 512 * cb:512 * cb + 512],
                      sqq[:, 512 * cb:512 * cb + 512], qo, ev, shf, t2)
            c.dma("gpsimd", out=KTO[128 * kt:128 * kt + 128, 512 * cb:512 * cb + 512],
                  in_=qo)
    ar.pop(); arR.pop()

    ar.push(); arR.push()
    amsk = ar.al(8 * 512)
    c.dma("sync", out=amsk, in_=I["amask"])
    vplain = ar.al(2048)
    rbc_sb = ar.al(512, parts=64)
    onescv = ar.al(32)
    onesrow = ar.al(4096, parts=1)
    negC = ar.al(8, parts=1)
    prod = ar.al(2048, parts=64)
    zcol = ar.al(8)
    c.add("vector", lambda e: e.memset(onescv, 1.0))
    c.add("vector", lambda e: e.memset(onesrow, 1.0))
    c.add("vector", lambda e: e.memset(negC[0:1, 0:1], -200.0))
    c.add("vector", lambda e: e.memset(zcol[:, 0:1], 0.0))
    ksb = arR.al(4096, parts=65)
    v65 = arR.al(32 * 65)
    qsb = arR.al(2048, parts=65)
    kto = arR.al(2048, parts=64)
    PT = arR.al(32 * 512)
    ysb = arR.al(512, parts=64)
    rrow = arR.al(512, parts=1)
    c.add("scalar", lambda e, d=v65.rearrange("p (c w) -> p c w", w=65)[:, :, 64:65],
          s=onescv.rearrange("p (c w) -> p c w", w=1):
          e.activation(d, s, AF.Copy, bias=0.0, scale=1.0))
    c.add("scalar", lambda e, d=ksb[64:65, :], s=onesrow:
          e.activation(d, s, AF.Copy, bias=0.0, scale=1.0))

    for h in range(KV):
        c.dma("sync", out=ksb[0:64, :], in_=KT[64 * h:64 * h + 64, :])
        c.dma("sync", out=kto, in_=KTO[64 * h:64 * h + 64, :])
        c.dma("sync", out=vplain, in_=V8[h].bitcast(F32))
        c.add("vector", lambda e, d=v65.rearrange("p (c w) -> p c w", w=65)[:, :, 0:64],
              s=vplain.rearrange("p (c w) -> p c w", w=64): e.tensor_copy(d, s))
        for g in range(4):
            c.dma("sync", out=qsb[0:64, :], in_=QT[256 * h + 64 * g:256 * h + 64 * g + 64, :])
            c.add("vector", lambda e: e.tensor_tensor(
                prod, qsb[0:64, :].bitcast(F32), kto.bitcast(F32), ALU.mult))
            for cc in range(4):
                c.add("tensor", lambda e, cc=cc: e.matmul(
                    psB[0:1, 512 * cc:512 * cc + 512], onescF[0:64, 0:1],
                    prod[:, 512 * cc:512 * cc + 512], start=True, stop=True))
            c.add("vector", lambda e: e.tensor_scalar(
                qsb[64:65, :], psB[0:1, 0:2048], -1.0, -200.0, ALU.mult, ALU.add))
            for sp in range(4):
                L = 8 * sp + 8
                qc = 512 * sp
                ng = L // 4
                for gr in range(ng):
                    for r4 in range(4):
                        kc = 4 * gr + r4
                        c.add("tensor", lambda e, kc=kc, r4=r4, qc=qc: e.matmul(
                            psA[:, 512 * r4:512 * r4 + 512],
                            ksb[:, 128 * kc:128 * kc + 128],
                            qsb[:, qc:qc + 512], start=True, stop=True))
                    base = 4 * gr - 8 * sp
                    if base >= 0:
                        c.add("vector", lambda e, base=base: e.tensor_tensor(
                            psA[:, 0:2048], psA[:, 0:2048],
                            amsk[:, 512 * base:512 * base + 2048], ALU.add))
                    c.add("scalar", lambda e, gr=gr: e.activation(
                        PT[:, 2048 * gr:2048 * gr + 2048], psA[:, 0:2048], AF.Exp,
                        bias=zcol[:, 0:1], scale=0.125))
                for kc in range(L):
                    c.add("tensor", lambda e, kc=kc, L=L: e.matmul(
                        psB[0:65, 0:512], v65[:, 65 * kc:65 * kc + 65],
                        PT[:, 512 * kc:512 * kc + 512],
                        start=(kc == 0), stop=(kc == L - 1)))
                c.add("vector", lambda e: e.reciprocal(rrow, psB[64:65, 0:512]))
                c.add("tensor", lambda e: e.matmul(psB[0:64, 512:1024], onesr[0:1, 0:64],
                                                   rrow, start=True, stop=True))
                c.add("scalar", lambda e: e.activation(rbc_sb, psB[0:64, 512:1024],
                                                       AF.Copy, bias=0.0, scale=1.0))
                c.add("vector", lambda e: e.tensor_tensor(ysb, psB[0:64, 0:512],
                                                          rbc_sb, ALU.mult))
                q0 = 4 * h + g
                for w in range(4):
                    bi = 4 * sp + w
                    c.dma("gpsimd",
                          out=yTB[bi][64 * (q0 % 2):64 * (q0 % 2) + 64,
                                      128 * (q0 // 2):128 * (q0 // 2) + 128],
                          in_=ysb[:, 128 * w:128 * w + 128])
    ar.pop(); arR.pop()

    ar.push(); arR.push()
    wo = arR.al(16 * 512)
    ytb = arR.al(16 * 128)
    bosb = arR.al(2048, parts=1)
    c.dma("sync", out=bosb, in_=I["bo"])
    ev = ar.al(512)
    for ec in range(4):
        c.dma("sync", out=wo, in_=I["WoTB"][ec])
        for tt in range(16):
            c.dma("sync", out=ytb, in_=yTB[tt])
            for fc in range(16):
                c.add("tensor", lambda e, fc=fc: e.matmul(
                    psA[:, 0:512], ytb[:, 128 * fc:128 * fc + 128],
                    wo[:, 512 * fc:512 * fc + 512],
                    start=(fc == 0), stop=False))
            c.add("tensor", lambda e, ec=ec: e.matmul(
                psA[:, 0:512], onesr, bosb[0:1, 512 * ec:512 * ec + 512],
                start=False, stop=True))
            c.add("scalar", lambda e: e.activation(ev, psA[:, 0:512], AF.Copy,
                                                   bias=0.0, scale=1.0))
            c.dma("gpsimd", out=oD[128 * tt:128 * tt + 128, 512 * ec:512 * ec + 512], in_=ev)
    ar.pop(); arR.pop()

    ln_pass(ctx, oD, I["x_own"], x1D)


def sbRbase(ar):
    return 0


def ln_pass(ctx, aD, bD, outD):
    ar, c = ctx["ar"], ctx["c"]
    epscol = ctx["epscol"]
    ar.push()
    ta = ar.al(2048)
    tb = ar.al(2048)
    r = ar.al(2048)
    sq = ar.al(2048)
    st = ar.al(8)
    for tt in range(16):
        c.dma("sync", out=ta, in_=aD[128 * tt:128 * tt + 128, :])
        c.dma("sync", out=tb, in_=bD[128 * tt:128 * tt + 128, :])
        c.add("vector", lambda e: e.tensor_tensor(r, ta, tb, ALU.add))
        c.add("vector", lambda e: e.tensor_reduce(st[:, 0:1], r, AX.X, ALU.add, False))
        c.add("vector", lambda e: e.tensor_scalar(st[:, 1:2], st[:, 0:1],
                                                  1.0 / 2048.0, None, ALU.mult))
        c.add("scalar", lambda e: e.activation(sq, r, AF.Square, bias=0.0, scale=1.0))
        c.add("vector", lambda e: e.tensor_reduce(st[:, 2:3], sq, AX.X, ALU.add, False))
        c.add("vector", lambda e: e.tensor_scalar(st[:, 3:4], st[:, 2:3],
                                                  1.0 / 2048.0, None, ALU.mult))
        c.add("vector", lambda e: e.tensor_tensor(st[:, 4:5], st[:, 1:2],
                                                  st[:, 1:2], ALU.mult))
        c.add("vector", lambda e: e.tensor_tensor(st[:, 5:6], st[:, 3:4],
                                                  st[:, 4:5], ALU.subtract))
        c.add("scalar", lambda e: e.activation(st[:, 6:7], st[:, 5:6], AF.Sqrt,
                                               bias=epscol[:, 0:1], scale=1.0))
        c.add("vector", lambda e: e.reciprocal(st[:, 7:8], st[:, 6:7]))
        c.add("vector", lambda e: e.tensor_scalar(ta, r, st[:, 1:2], st[:, 7:8],
                                                  ALU.subtract, ALU.mult))
        if tt == 0:
            c.dma("gpsimd", out=ctx["scr"]["dbgST"], in_=st)
            c.dma("gpsimd", out=ctx["scr"]["dbgR"], in_=r)
        c.dma("gpsimd", out=outD[128 * tt:128 * tt + 128, :], in_=ta)
    ar.pop()


def stage_B(ctx):
    I, scr, ar, arR, c = ctx["I"], ctx["scr"], ctx["ar"], ctx["arR"], ctx["c"]
    psA, psB = ctx["psA"], ctx["psB"]
    eyesb, onesr, onesrF, onescF = ctx["eyesb"], ctx["onesr"], ctx["onesrF"], ctx["onescF"]
    x1D, xnD, xqTB, o2D, x2D = scr["x1D"], scr["xnD"], scr["xqTB"], scr["o2D"], scr["x2D"]
    mx_i, mx_o = scr["mx_i"], scr["mx_o"]

    ar.push(); arR.push()
    wt = ar.al(2048)
    wsum = ar.al(8)
    sc = ar.al(8, parts=1)
    for ec in range(16):
        c.dma("sync", out=wt, in_=I["WfT"][128 * ec:128 * ec + 128, :])
        c.add("vector", lambda e, ec=ec: e.tensor_reduce(wsum[:, 1:2], wt, AX.X,
                                                         ALU.add, True))
        if ec == 0:
            c.add("vector", lambda e: e.tensor_copy(wsum[:, 0:1], wsum[:, 1:2]))
        else:
            c.add("vector", lambda e: e.tensor_tensor(wsum[:, 0:1], wsum[:, 0:1],
                                                      wsum[:, 1:2], ALU.add))
    c.add("tensor", lambda e: e.matmul(psB[0:1, 0:1], wsum[:, 0:1], onescF[:, 0:1],
                                       start=True, stop=True))
    c.add("vector", lambda e: e.tensor_scalar(sc[0:1, 0:1], psB[0:1, 0:1],
                                              1.0 / (2048.0 * 2048.0), 1e-5,
                                              ALU.mult, ALU.max))
    c.add("vector", lambda e: e.reciprocal(sc[0:1, 1:2], sc[0:1, 0:1]))

    gsb = ar.al(2048, parts=1)
    c.dma("sync", out=gsb, in_=I["gamma"])
    gbc = ar.al(2048)
    for cc in range(4):
        c.add("tensor", lambda e, cc=cc: e.matmul(
            psA[:, 512 * cc:512 * cc + 512], onesrF, gsb[0:1, 512 * cc:512 * cc + 512],
            start=True, stop=True))
    c.add("scalar", lambda e: e.activation(gbc, psA[:, 0:2048], AF.Copy, bias=0.0, scale=1.0))
    xt = ar.al(2048)
    sq = ar.al(2048)
    st = ar.al(8)
    gmax = ar.al(8)
    eps6 = ar.al(8)
    c.add("vector", lambda e: e.memset(eps6[:, 0:1], 1e-6))
    xn = ar.al(2048)
    for tt in range(16):
        c.dma("sync", out=xt, in_=x1D[128 * tt:128 * tt + 128, :])
        c.add("scalar", lambda e: e.activation(sq, xt, AF.Square, bias=0.0, scale=1.0))
        c.add("vector", lambda e: e.tensor_reduce(st[:, 0:1], sq, AX.X, ALU.add, False))
        c.add("vector", lambda e: e.tensor_scalar(st[:, 1:2], st[:, 0:1],
                                                  1.0 / 2048.0, None, ALU.mult))
        c.add("scalar", lambda e: e.activation(st[:, 2:3], st[:, 1:2], AF.Sqrt,
                                               bias=eps6[:, 0:1], scale=1.0))
        c.add("vector", lambda e: e.reciprocal(st[:, 3:4], st[:, 2:3]))
        c.add("vector", lambda e: e.scalar_tensor_tensor(xn, xt, st[:, 3:4], gbc,
                                                         ALU.mult, ALU.mult))
        c.dma("gpsimd", out=xnD[128 * tt:128 * tt + 128, :], in_=xn)
        c.add("vector", lambda e: e.tensor_reduce(st[:, 4:5], xn, AX.X, ALU.max, True))
        if tt == 0:
            c.add("vector", lambda e: e.tensor_copy(gmax[:, 0:1], st[:, 4:5]))
        else:
            c.add("vector", lambda e: e.tensor_tensor(gmax[:, 0:1], gmax[:, 0:1],
                                                      st[:, 4:5], ALU.max))
    c.add("tensor", lambda e: e.transpose(psB[0:1, 0:128], gmax[:, 0:1], eyesb))
    c.add("vector", lambda e: e.tensor_reduce(sc[0:1, 2:3], psB[0:1, 0:128],
                                              AX.X, ALU.max, False))
    c.dma("gpsimd", out=mx_i[0:1, 0:1], in_=sc[0:1, 2:3])
    c.coll("AllReduce", ALU.max, [[0, 1, 2, 3, 4, 5, 6, 7]], [mx_i], [mx_o])
    c.dma("sync", out=sc[0:1, 3:4], in_=mx_o[0:1, 0:1])
    c.add("vector", lambda e: e.tensor_scalar(sc[0:1, 3:4], sc[0:1, 3:4], 1e-5,
                                              None, ALU.max))
    c.add("vector", lambda e: e.reciprocal(sc[0:1, 4:5], sc[0:1, 3:4]))
    c.add("vector", lambda e: e.tensor_scalar(sc[0:1, 4:5], sc[0:1, 4:5], 127.0,
                                              None, ALU.mult))
    c.add("vector", lambda e: e.tensor_tensor(sc[0:1, 5:6], sc[0:1, 3:4],
                                              sc[0:1, 0:1], ALU.mult))
    c.add("vector", lambda e: e.tensor_scalar(sc[0:1, 5:6], sc[0:1, 5:6],
                                              1.0 / 127.0, None, ALU.mult))
    kcol = ar.al(8)
    occol = ar.al(8)
    iwcol = ar.al(8)
    c.add("tensor", lambda e: e.matmul(psB[:, 128:129], onesrF, sc[0:1, 4:5],
                                       start=True, stop=True))
    c.add("vector", lambda e: e.tensor_copy(kcol[:, 0:1], psB[:, 128:129]))
    c.add("tensor", lambda e: e.matmul(psB[:, 129:130], onesrF, sc[0:1, 5:6],
                                       start=True, stop=True))
    c.add("vector", lambda e: e.tensor_copy(occol[:, 0:1], psB[:, 129:130]))
    c.add("tensor", lambda e: e.matmul(psB[:, 130:131], onesrF, sc[0:1, 1:2],
                                       start=True, stop=True))
    c.add("vector", lambda e: e.tensor_copy(iwcol[:, 0:1], psB[:, 130:131]))

    q1 = ar.al(2048)
    rb = ar.al(2048)
    cb_ = ar.al(2048)
    xqR = arR.al(2048)
    for tt in range(16):
        c.dma("sync", out=xt, in_=xnD[128 * tt:128 * tt + 128, :])
        c.add("vector", lambda e: e.tensor_scalar(q1, xt, kcol[:, 0:1], None, ALU.mult))
        c.add("vector", lambda e: e.tensor_scalar(q1, q1, 127.0, -128.0, ALU.min, ALU.max))
        c.add("vector", lambda e: e.tensor_scalar(q1, q1, 128.5, None, ALU.add))
        c.add("vector", lambda e: e.memset(sq, 0.0))
        for b in [128.0, 64.0, 32.0, 16.0, 8.0, 4.0, 2.0, 1.0]:
            c.add("vector", lambda e, b=b: e.tensor_scalar(rb, sq, b, None, ALU.add))
            c.add("vector", lambda e: e.tensor_tensor(cb_, q1, rb, ALU.is_ge))
            c.add("vector", lambda e, b=b: e.scalar_tensor_tensor(sq, cb_, b, sq,
                                                                  ALU.mult, ALU.add))
        c.add("vector", lambda e: e.tensor_scalar(sq, sq, 128.0, None, ALU.subtract))
        for ec in range(16):
            g = ec // 4
            pst = psA if g % 2 == 0 else psB
            c.add("tensor", lambda e, d=pst[:, 128 * (ec % 4):128 * (ec % 4) + 128],
                  s=sq[:, 128 * ec:128 * ec + 128], ey=eyesb: e.transpose(d, s, ey))
            if ec % 4 == 3:
                c.add("scalar", lambda e, d=xqR[:, 512 * g:512 * g + 512],
                      s=pst[:, 0:512]: e.activation(d, s, AF.Copy, bias=0.0, scale=1.0))
        c.dma("gpsimd", out=xqTB[tt].bitcast(F32), in_=xqR.bitcast(F32))
    ar.pop(); arR.pop()

    ar.push(); arR.push()
    wt4 = ar.al(512)
    wn4 = ar.al(512)
    a4 = ar.al(512)
    b4 = ar.al(512)
    osb = ar.al(512)
    wqR = arR.al(16 * 512)
    xqt = arR.al(2048)
    iw2 = ar.al(8)
    oc2 = ar.al(8)
    c.add("vector", lambda e: e.tensor_copy(iw2[:, 0:1], iwcol[:, 0:1]))
    c.add("vector", lambda e: e.tensor_copy(oc2[:, 0:1], occol[:, 0:1]))
    for oc in range(4):
        for ec in range(16):
            c.dma("sync", out=wt4,
                  in_=I["WfT"][128 * ec:128 * ec + 128, 512 * oc:512 * oc + 512])
            c.add("vector", lambda e: e.tensor_scalar(wn4, wt4, iw2[:, 0:1], None,
                                                      ALU.mult))
            c.add("vector", lambda e: e.tensor_scalar(a4, wn4, 0.5, None, ALU.is_ge))
            c.add("vector", lambda e: e.tensor_scalar(b4, wn4, -0.5, None, ALU.is_le))
            c.add("vector", lambda e, d=wqR[:, 512 * ec:512 * ec + 512]:
                  e.tensor_tensor(d, a4, b4, ALU.subtract))
        for tt in range(16):
            c.dma("sync", out=xqt.bitcast(F32), in_=xqTB[tt].bitcast(F32))
            for ec in range(16):
                c.add("tensor", lambda e, l=xqt[:, 128 * ec:128 * ec + 128],
                      r=wqR[:, 512 * ec:512 * ec + 512],
                      st=(ec == 0), sp=(ec == 15): e.matmul(psA[:, 0:512], l, r,
                                                            start=st, stop=sp))
            c.add("scalar", lambda e: e.activation(osb, psA[:, 0:512], AF.Copy,
                                                   bias=0.0, scale=oc2[:, 0:1]))
            c.dma("gpsimd", out=o2D[128 * tt:128 * tt + 128, 512 * oc:512 * oc + 512],
                  in_=osb)
    ar.pop(); arR.pop()

    ln_pass(ctx, o2D, x1D, x2D)


def stage_C(ctx):
    I, scr, ar, arR, c = ctx["I"], ctx["scr"], ctx["ar"], ctx["arR"], ctx["c"]
    psA, psB = ctx["psA"], ctx["psB"]
    eyesb, onesr, onesrF, onescF = ctx["eyesb"], ctx["onesr"], ctx["onesrF"], ctx["onescF"]
    x2D, x3D = scr["x2D"], scr["x3D"]
    ag_i, ag_o, x2T, prjD = scr["ag_i"], scr["ag_o"], scr["x2T"], scr["prjD"]
    xcD, yMD, gD, gnB = scr["xcD"], scr["yMD"], scr["gD"], scr["gnB"]
    ss_i, ss_o, rs_i, rs_o = scr["ss_i"], scr["ss_o"], scr["rs_i"], scr["rs_o"]

    c.dma("sync", out=ag_i, in_=x2D)
    c.coll("AllGather", ALU.bypass, [[0, 1], [2, 3], [4, 5], [6, 7]], [ag_i], [ag_o])

    ar.push(); arR.push()
    xtl = ar.al(2048)
    xev = arR.al(512)
    for p in range(32):
        c.dma("sync", out=xtl, in_=ag_o[p % 2][128 * (p // 2):128 * (p // 2) + 128, :])
        for g4 in range(4):
            pst = psA if g4 % 2 == 0 else psB
            for i in range(4):
                ec = 4 * g4 + i
                c.add("tensor", lambda e, d=pst[:, 128 * i:128 * i + 128],
                      s=xtl[:, 128 * ec:128 * ec + 128], ey=eyesb: e.transpose(d, s, ey))
            c.add("scalar", lambda e, d=xev, s=pst[:, 0:512]:
                  e.activation(d, s, AF.Copy, bias=0.0, scale=1.0))
            for i in range(4):
                ec = 4 * g4 + i
                c.dma("gpsimd", out=x2T[128 * ec:128 * ec + 128, 128 * p:128 * p + 128],
                      in_=xev[:, 128 * i:128 * i + 128])
    ar.pop(); arR.pop()

    ar.push(); arR.push()
    xcol = arR.al(16 * 512)
    wsl = arR.al(2048)
    pev = ar.al(512)
    for tc in range(8):
        for i in range(16):
            c.dma("sync", out=xcol[:, 512 * i:512 * i + 512],
                  in_=x2T[128 * i:128 * i + 128, 512 * tc:512 * tc + 512])
        for pr in range(35):
            c.dma("sync", out=wsl, in_=I["WinTB"][pr])
            for i in range(16):
                c.add("tensor", lambda e, l=wsl[:, 128 * i:128 * i + 128],
                      r=xcol[:, 512 * i:512 * i + 512],
                      st=(i == 0), sp=(i == 15): e.matmul(psA[:, 0:512], l, r,
                                                          start=st, stop=sp))
            c.add("scalar", lambda e: e.activation(pev, psA[:, 0:512], AF.Copy,
                                                   bias=0.0, scale=1.0))
            c.dma("gpsimd", out=prjD[128 * pr:128 * pr + 128, 512 * tc:512 * tc + 512],
                  in_=pev)
    ar.pop(); arR.pop()

    ar.push(); arR.push()
    cwsb = ar.al(72)
    cbsb = ar.al(18)
    c.dma("sync", out=cwsb, in_=I["convw"])
    c.dma("sync", out=cbsb, in_=I["convb"])
    xcv = ar.al(4104)
    acc = ar.al(4096)
    xsl = arR.al(4096)
    for ct in range(18):
        c.add("vector", lambda e: e.memset(xcv[:, 0:3], 0.0))
        c.dma("sync", out=xcv[:, 3:4099], in_=prjD[2048 + 128 * ct:2048 + 128 * ct + 128, :])
        c.add("vector", lambda e, ct=ct: e.tensor_scalar(
            acc, xcv[:, 0:4096], cwsb[:, 4 * ct:4 * ct + 1], cbsb[:, ct:ct + 1],
            ALU.mult, ALU.add))
        for j in range(1, 4):
            c.add("vector", lambda e, ct=ct, j=j: e.scalar_tensor_tensor(
                acc, xcv[:, j:j + 4096], cwsb[:, 4 * ct + j:4 * ct + j + 1], acc,
                ALU.mult, ALU.add))
        c.add("scalar", lambda e: e.activation(xsl, acc, AF.Silu, bias=0.0, scale=1.0))
        c.dma("gpsimd", out=xcD[128 * ct:128 * ct + 128, :], in_=xsl)
    ar.pop(); arR.pop()

    ar.push(); arR.push()
    trisb = ar.al(512)
    c.dma("sync", out=trisb, in_=I["triu"])
    col32 = ar.al(8, parts=32)
    Acol = ar.al(8, parts=32)
    dtp = ar.al(4096, parts=32)
    tm1 = ar.al(4096, parts=32)
    c.dma("sync", out=dtp, in_=prjD[4352:4384, :])
    c.dma("sync", out=col32[:, 0:1], in_=I["dtb"])
    c.dma("sync", out=col32[:, 1:2], in_=I["Alog"])
    c.solo = True
    c.add("vector", lambda e: e.tensor_scalar(dtp, dtp, col32[:, 0:1], None, ALU.add))
    c.add("scalar", lambda e: e.activation(tm1, dtp, AF.Abs, bias=0.0, scale=1.0))
    c.add("scalar", lambda e: e.activation(tm1, tm1, AF.Exp, bias=0.0, scale=-1.0))
    c.add("vector", lambda e: e.tensor_scalar(tm1, tm1, 1.0, None, ALU.add))
    c.add("scalar", lambda e: e.activation(tm1, tm1, AF.Ln, bias=0.0, scale=1.0))
    c.add("scalar", lambda e: e.activation(dtp, dtp, AF.Relu, bias=0.0, scale=1.0))
    c.add("vector", lambda e: e.tensor_tensor(dtp, dtp, tm1, ALU.add))
    c.add("scalar", lambda e: e.activation(Acol[:, 0:1], col32[:, 1:2], AF.Exp,
                                           bias=0.0, scale=1.0))
    c.add("vector", lambda e: e.tensor_scalar(Acol[:, 1:2], Acol[:, 0:1], -1.0,
                                              None, ALU.mult))
    c.add("vector", lambda e: e.tensor_scalar(tm1, dtp, Acol[:, 1:2], None, ALU.mult))
    c.solo = False
    AcumC = ar.al(256, parts=32)
    expAC = ar.al(256, parts=32)
    dec32 = ar.al(256, parts=32)
    dAT32 = ar.al(64)
    AcumT = ar.al(64)
    dtT = ar.al(64)
    decT = ar.al(64)
    dtdecT = ar.al(64)
    edlbc = ar.al(32)
    erow = ar.al(32, parts=1)
    arow = ar.al(256, parts=1)
    erowh = ar.al(256, parts=1)
    bT = arR.al(256)
    cTt = arR.al(256)
    btok = arR.al(256)
    cbm = ar.al(512)
    argm = ar.al(256)
    tl = ar.al(256)
    Mt = arR.al(512)
    xcvf = ar.al(256, parts=64)
    xdt = arR.al(64)
    xdtd = arR.al(64)
    eab = ar.al(256, parts=64)
    t1s = ar.al(256, parts=64)
    y2s = ar.al(256, parts=64)
    y3s = ar.al(256, parts=64)
    SS = arR.al(2048)
    SSf = ar.al(2048)
    drs = ar.al(32, parts=64)
    c.dma("sync", out=drs, in_=I["Drep"])
    c.add("vector", lambda e: e.memset(SSf, 0.0))
    c.add("scalar", lambda e: e.activation(SS, SSf, AF.Copy, bias=0.0, scale=1.0))
    for ci in range(16):
        for sc in range(2):
            c.add("tensor", lambda e, s=tm1[:, 256 * ci + 128 * sc:256 * ci + 128 * sc + 128],
                  ey=eyesb[0:32, 0:32]: e.transpose(psA[:, 0:32], s, ey))
            c.add("vector", lambda e, d=dAT32[:, 32 * sc:32 * sc + 32]:
                  e.tensor_copy(d, psA[:, 0:32]))
        for sc in range(2):
            c.add("tensor", lambda e, l=dAT32[:, 32 * sc:32 * sc + 32],
                  r=trisb[:, 256 * sc:256 * sc + 256], st=(sc == 0), sp=(sc == 1):
                  e.matmul(psB[0:32, 0:256], l, r, start=st, stop=sp))
        c.solo = True
        c.add("scalar", lambda e: e.activation(AcumC, psB[0:32, 0:256], AF.Copy,
                                               bias=0.0, scale=1.0))
        c.add("scalar", lambda e: e.activation(expAC, AcumC, AF.Exp, bias=0.0, scale=1.0))
        c.add("scalar", lambda e: e.activation(dec32, AcumC, AF.Exp,
                                               bias=AcumC[:, 255:256], scale=-1.0))
        c.add("scalar", lambda e: e.activation(col32[:, 2:3], AcumC[:, 255:256],
                                               AF.Exp, bias=0.0, scale=1.0))
        c.solo = False
        for sc in range(2):
            c.add("tensor", lambda e, s=AcumC[:, 128 * sc:128 * sc + 128],
                  ey=eyesb[0:32, 0:32]: e.transpose(psA[:, 0:32], s, ey))
            c.add("vector", lambda e, d=AcumT[:, 32 * sc:32 * sc + 32]:
                  e.tensor_copy(d, psA[:, 0:32]))
            c.add("tensor", lambda e,
                  s=dtp[:, 256 * ci + 128 * sc:256 * ci + 128 * sc + 128],
                  ey=eyesb[0:32, 0:32]: e.transpose(psA[:, 32:64], s, ey))
            c.add("vector", lambda e, d=dtT[:, 32 * sc:32 * sc + 32]:
                  e.tensor_copy(d, psA[:, 32:64]))
            c.add("tensor", lambda e, s=dec32[:, 128 * sc:128 * sc + 128],
                  ey=eyesb[0:32, 0:32]: e.transpose(psA[:, 64:96], s, ey))
            c.add("vector", lambda e, d=decT[:, 32 * sc:32 * sc + 32]:
                  e.tensor_copy(d, psA[:, 64:96]))
        c.add("vector", lambda e: e.tensor_tensor(dtdecT, dtT, decT, ALU.mult))
        c.add("tensor", lambda e: e.transpose(psA[0:1, 96:128], col32[:, 2:3],
                                              eyesb[0:32, 0:32]))
        c.add("vector", lambda e: e.tensor_copy(erow[0:1, 0:32], psA[0:1, 96:128]))
        c.add("tensor", lambda e: e.matmul(psA[:, 128:160], onesrF, erow,
                                           start=True, stop=True))
        c.add("vector", lambda e: e.tensor_copy(edlbc, psA[:, 128:160]))
        c.dma("sync", out=bT, in_=xcD[2048:2176, 256 * ci:256 * ci + 256])
        c.dma("sync", out=cTt, in_=xcD[2176:2304, 256 * ci:256 * ci + 256])
        for sc in range(2):
            c.add("tensor", lambda e, s=bT[:, 128 * sc:128 * sc + 128].bitcast(F32),
                  ey=eyesb: e.transpose(psA[:, 0:128], s, ey))
            c.add("scalar", lambda e, d=btok[:, 128 * sc:128 * sc + 128]:
                  e.activation(d, psA[:, 0:128], AF.Copy, bias=0.0, scale=1.0))
            c.add("tensor", lambda e, l=bT[:, 128 * sc:128 * sc + 128], r=cTt:
                  e.matmul(psA[:, 256:512], l, r, start=True, stop=True))
            c.add("vector", lambda e, d=cbm[:, 256 * sc:256 * sc + 256], sc=sc:
                  e.tensor_tensor(d, psA[:, 256:512],
                                  trisb[:, 256 * sc:256 * sc + 256], ALU.mult))
        for h in range(NHh):
            c.dma("sync", out=arow, in_=AcumC[h:h + 1, 0:256])
            c.dma("sync", out=erowh, in_=expAC[h:h + 1, 0:256])
            for sc in range(2):
                c.add("tensor", lambda e: e.matmul(
                    psB[:, 1536:1792], onesrF, arow, start=True, stop=True))
                c.add("vector", lambda e, col=AcumT[:, 32 * sc + h:32 * sc + h + 1]:
                      e.tensor_scalar(argm, psB[:, 1536:1792], col, 0.0,
                                      ALU.subtract, ALU.min))
                c.add("scalar", lambda e: e.activation(tl, argm, AF.Exp,
                                                       bias=0.0, scale=1.0))
                c.add("vector", lambda e, d=Mt[:, 256 * sc:256 * sc + 256],
                      s2=cbm[:, 256 * sc:256 * sc + 256]:
                      e.tensor_tensor(d, tl, s2, ALU.mult))
            c.dma("sync", out=xcvf,
                  in_=xcD[64 * h:64 * h + 64, 256 * ci:256 * ci + 256].bitcast(F32))
            for sc in range(2):
                c.add("tensor", lambda e, s=xcvf[:, 128 * sc:128 * sc + 128],
                      ey=eyesb[0:64, 0:64]: e.transpose(psA[:, 640:704], s, ey))
                c.add("vector", lambda e, col=dtT[:, 32 * sc + h:32 * sc + h + 1]:
                      e.tensor_scalar(xdt, psA[:, 640:704], col, None, ALU.mult))
                c.add("vector", lambda e, col=dtdecT[:, 32 * sc + h:32 * sc + h + 1]:
                      e.tensor_scalar(xdtd, psA[:, 640:704], col, None, ALU.mult))
                c.add("tensor", lambda e, l=xdt, r=Mt[:, 256 * sc:256 * sc + 256],
                      st=(sc == 0), sp=(sc == 1): e.matmul(psB[0:64, 0:256], l, r,
                                                           start=st, stop=sp))
                c.add("tensor", lambda e, l=btok[:, 128 * sc:128 * sc + 128], r=xdtd,
                      st=(sc == 0), sp=(sc == 1): e.matmul(psB[:, 512:576], l, r,
                                                           start=st, stop=sp))
            c.add("tensor", lambda e, l=SS[:, 64 * h:64 * h + 64], r=cTt:
                  e.matmul(psB[0:64, 1024:1280], l, r, start=True, stop=True))
            c.add("tensor", lambda e: e.matmul(
                psA[0:64, 1536:1792], onesrF[0:1, 0:64], erowh,
                start=True, stop=True))
            c.add("scalar", lambda e: e.activation(eab, psA[0:64, 1536:1792],
                                                   AF.Copy, bias=0.0, scale=1.0))
            c.add("vector", lambda e: e.tensor_tensor(t1s, psB[0:64, 1024:1280],
                                                      eab, ALU.mult))
            c.add("vector", lambda e, h=h: e.scalar_tensor_tensor(
                y2s, xcvf, drs[:, h:h + 1], t1s, ALU.mult, ALU.add))
            c.add("vector", lambda e: e.tensor_tensor(y3s, y2s, psB[0:64, 0:256],
                                                      ALU.add))
            c.dma("gpsimd", out=yMD[64 * h:64 * h + 64, 256 * ci:256 * ci + 256],
                  in_=y3s)
            c.add("vector", lambda e, h=h: e.scalar_tensor_tensor(
                SS[:, 64 * h:64 * h + 64], SS[:, 64 * h:64 * h + 64],
                edlbc[:, h:h + 1], psB[:, 512:576], ALU.mult, ALU.add))
    ar.pop(); arR.pop()

    ar.push(); arR.push()
    yt5 = ar.al(4096)
    zt5 = ar.al(4096)
    for ct in range(16):
        c.dma("sync", out=yt5, in_=yMD[128 * ct:128 * ct + 128, :])
        c.dma("sync", out=zt5, in_=prjD[128 * ct:128 * ct + 128, :])
        c.add("scalar", lambda e: e.activation(zt5, zt5, AF.Silu, bias=0.0, scale=1.0))
        c.add("vector", lambda e: e.tensor_tensor(yt5, yt5, zt5, ALU.mult))
        c.add("scalar", lambda e: e.activation(zt5, yt5, AF.Square, bias=0.0, scale=1.0))
        for cc in range(8):
            pst = psA if cc < 4 else psB
            c.add("tensor", lambda e, d=pst[0:1, 512 * (cc % 4):512 * (cc % 4) + 512],
                  s=zt5[:, 512 * cc:512 * cc + 512], st=(ct == 0), sp=(ct == 15):
                  e.matmul(d, onescF[:, 0:1], s, start=st, stop=sp))
        c.dma("gpsimd", out=gD[128 * ct:128 * ct + 128, :], in_=yt5)
    ssrow = ar.al(4096, parts=1)
    c.add("scalar", lambda e: e.activation(ssrow[0:1, 0:2048], psA[0:1, 0:2048],
                                           AF.Copy, bias=0.0, scale=1.0))
    c.add("scalar", lambda e: e.activation(ssrow[0:1, 2048:4096], psB[0:1, 0:2048],
                                           AF.Copy, bias=0.0, scale=1.0))
    c.dma("gpsimd", out=ss_i, in_=ssrow)
    c.coll("AllReduce", ALU.add, [[0, 1], [2, 3], [4, 5], [6, 7]], [ss_i], [ss_o])
    c.dma("sync", out=ssrow, in_=ss_o)
    c.add("vector", lambda e: e.tensor_scalar(ssrow, ssrow, 1.0 / 4096.0, 1e-5,
                                              ALU.mult, ALU.add))
    c.add("scalar", lambda e: e.activation(ssrow, ssrow, AF.Sqrt, bias=0.0, scale=1.0))
    c.add("vector", lambda e: e.reciprocal(ssrow, ssrow))
    rsbc = ar.al(4096)
    for cc in range(8):
        pst = psA if cc < 4 else psB
        c.add("tensor", lambda e, d=pst[:, 512 * (cc % 4):512 * (cc % 4) + 512],
              s=ssrow[0:1, 512 * cc:512 * cc + 512]: e.matmul(d, onesrF, s,
                                                              start=True, stop=True))
    c.add("scalar", lambda e: e.activation(rsbc[:, 0:2048], psA[:, 0:2048],
                                           AF.Copy, bias=0.0, scale=1.0))
    c.add("scalar", lambda e: e.activation(rsbc[:, 2048:4096], psB[:, 0:2048],
                                           AF.Copy, bias=0.0, scale=1.0))
    nwsb = ar.al(16)
    c.dma("sync", out=nwsb, in_=I["nw"])
    gnR = arR.al(4096)
    for ct in range(16):
        c.dma("sync", out=yt5, in_=gD[128 * ct:128 * ct + 128, :])
        c.add("vector", lambda e, ct=ct: e.scalar_tensor_tensor(
            gnR, yt5, nwsb[:, ct:ct + 1], rsbc, ALU.mult, ALU.mult))
        for tw in range(32):
            c.dma("gpsimd", out=gnB[tw][:, 128 * ct:128 * ct + 128],
                  in_=gnR[:, 128 * tw:128 * tw + 128])
    ar.pop(); arR.pop()
    ar.push(); arR.push()
    wo5 = arR.al(16 * 512)
    gblk = arR.al(2048)
    oev = ar.al(512)
    for ec in range(4):
        c.dma("sync", out=wo5, in_=I["WoutTB"][ec])
        for tw in range(32):
            c.dma("sync", out=gblk, in_=gnB[tw])
            for i in range(16):
                c.add("tensor", lambda e, l=gblk[:, 128 * i:128 * i + 128],
                      r=wo5[:, 512 * i:512 * i + 512],
                      st=(i == 0), sp=(i == 15): e.matmul(psA[:, 0:512], l, r,
                                                          start=st, stop=sp))
            c.add("scalar", lambda e: e.activation(oev, psA[:, 0:512], AF.Copy,
                                                   bias=0.0, scale=1.0))
            slot = (tw % 2) * 16 + tw // 2
            c.dma("gpsimd", out=rs_i[128 * slot:128 * slot + 128, 512 * ec:512 * ec + 512],
                  in_=oev)
    ar.pop(); arR.pop()
    c.coll("ReduceScatter", ALU.add, [[0, 1], [2, 3], [4, 5], [6, 7]], [rs_i], [rs_o])
    ln_pass(ctx, rs_o, x2D, x3D)


def stage_D(ctx):
    I, scr, ar, arR, c = ctx["I"], ctx["scr"], ctx["ar"], ctx["arR"], ctx["c"]
    psA, psB = ctx["psA"], ctx["psB"]
    eyesb, onesr, onesrF, onescF = ctx["eyesb"], ctx["onesr"], ctx["onesrF"], ctx["onescF"]
    nc = ctx["nc"]
    x3D, o2D, out_own = scr["x3D"], scr["o2D"], scr["out_own"]
    x3T, mixD, tD, TaD = scr["x3T"], scr["mixD"], scr["tD"], scr["TaD"]
    tm_i, tm_o, ts_i, ts_o = scr["tm_i"], scr["tm_o"], scr["ts_i"], scr["ts_o"]

    ar.push(); arR.push()
    xtl = ar.al(2048)
    xev = arR.al(512)
    for p in range(16):
        c.dma("sync", out=xtl, in_=x3D[128 * p:128 * p + 128, :])
        for g4 in range(4):
            pst = psA if g4 % 2 == 0 else psB
            for i in range(4):
                ec = 4 * g4 + i
                c.add("tensor", lambda e, d=pst[:, 128 * i:128 * i + 128],
                      s=xtl[:, 128 * ec:128 * ec + 128], ey=eyesb: e.transpose(d, s, ey))
            c.add("scalar", lambda e, d=xev, s=pst[:, 0:512]:
                  e.activation(d, s, AF.Copy, bias=0.0, scale=1.0))
            for i in range(4):
                ec = 4 * g4 + i
                c.dma("gpsimd", out=x3T[128 * ec:128 * ec + 128, 128 * p:128 * p + 128],
                      in_=xev[:, 128 * i:128 * i + 128])
    ar.pop(); arR.pop()

    ar.push(); arR.push()
    gsb = arR.al(128)
    c.dma("sync", out=gsb, in_=I["gwTB"])
    xcol = arR.al(16 * 512)
    lgT = ar.al(2048, parts=8)
    lg = ar.al(128)
    topv = ar.al(32)
    topi = ar.al(32)
    vals8 = ar.al(8)
    if8 = ar.al(8)
    tvT = ar.al(128, parts=32)
    tcol = ar.al(8, parts=32)
    trow = ar.al(32, parts=1)
    gmrow = ar.al(8, parts=1)
    nmb = ar.al(8)
    exw = ar.al(32)
    wv = ar.al(32)
    rzb = ar.al(8)
    tmp1 = ar.al(8)
    tmp2 = ar.al(8)
    cesb = ar.al(128)
    for tc in range(4):
        for i in range(16):
            c.dma("sync", out=xcol[:, 512 * i:512 * i + 512],
                  in_=x3T[128 * i:128 * i + 128, 512 * tc:512 * tc + 512])
        for i in range(16):
            c.add("tensor", lambda e, l=gsb[:, 8 * i:8 * i + 8],
                  r=xcol[:, 512 * i:512 * i + 512], st=(i == 0), sp=(i == 15):
                  e.matmul(psA[0:8, 0:512], l, r, start=st, stop=sp))
        c.add("scalar", lambda e, d=lgT[:, 512 * tc:512 * tc + 512]:
              e.activation(d, psA[0:8, 0:512], AF.Copy, bias=0.0, scale=1.0))
    for tc in range(4):
        for q in range(4):
            c.add("tensor", lambda e, s=lgT[:, 512 * tc + 128 * q:512 * tc + 128 * q + 128],
                  ey=eyesb[0:8, 0:8]: e.transpose(psB[:, 0:8], s, ey))
            c.add("vector", lambda e, d=lg[:, 8 * (4 * tc + q):8 * (4 * tc + q) + 8]:
                  e.tensor_copy(d, psB[:, 0:8]))
    iu32 = ctx["idxu"]
    for tt in range(16):
        c.add("vector", lambda e, s=lg[:, 8 * tt:8 * tt + 8]:
              e.max_with_indices(vals8, iu32[:, 0:8], s))
        c.add("vector", lambda e: e.tensor_copy(if8, iu32[:, 0:8]))
        for k in range(2):
            c.add("vector", lambda e, d=topv[:, 16 * k + tt:16 * k + tt + 1], k=k:
                  e.tensor_copy(d, vals8[:, k:k + 1]))
            c.add("vector", lambda e, d=topi[:, 16 * k + tt:16 * k + tt + 1], k=k:
                  e.tensor_copy(d, if8[:, k:k + 1]))
    c.add("tensor", lambda e: e.transpose(psA[0:32, 0:128], topv, eyesb))
    c.add("vector", lambda e: e.tensor_copy(tvT, psA[0:32, 0:128]))
    c.add("vector", lambda e: e.tensor_reduce(tcol[:, 0:1], tvT, AX.X, ALU.max, False))
    c.add("tensor", lambda e: e.transpose(psB[0:1, 0:32], tcol[:, 0:1], eyesb[0:32, 0:32]))
    c.add("vector", lambda e: e.tensor_copy(trow, psB[0:1, 0:32]))
    c.add("vector", lambda e: e.tensor_reduce(gmrow[0:1, 0:1], trow[0:1, 0:16],
                                              AX.X, ALU.max, False))
    c.add("vector", lambda e: e.tensor_reduce(gmrow[0:1, 1:2], trow[0:1, 16:32],
                                              AX.X, ALU.max, False))
    c.dma("gpsimd", out=tm_i[0:1, 0:2], in_=gmrow[0:1, 0:2])
    c.coll("AllReduce", ALU.max, [[0, 1], [2, 3], [4, 5], [6, 7]], [tm_i], [tm_o])
    c.dma("sync", out=gmrow[0:1, 2:4], in_=tm_o[0:1, 0:2])
    c.add("vector", lambda e: e.tensor_scalar(gmrow[0:1, 4:6], gmrow[0:1, 2:4],
                                              -1.0, None, ALU.mult))
    c.add("tensor", lambda e: e.matmul(psB[:, 0:2], onesrF, gmrow[0:1, 4:6],
                                       start=True, stop=True))
    c.add("vector", lambda e: e.tensor_copy(nmb[:, 0:2], psB[:, 0:2]))
    for k in range(2):
        c.add("scalar", lambda e, k=k: e.activation(
            exw[:, 16 * k:16 * k + 16], topv[:, 16 * k:16 * k + 16], AF.Exp,
            bias=nmb[:, k:k + 1], scale=1.0))
    c.add("tensor", lambda e: e.transpose(psA[0:32, 128:256], exw, eyesb))
    c.add("vector", lambda e: e.tensor_copy(tvT, psA[0:32, 128:256]))
    c.add("vector", lambda e: e.tensor_reduce(tcol[:, 1:2], tvT, AX.X, ALU.add, False))
    c.add("tensor", lambda e: e.transpose(psB[0:1, 32:64], tcol[:, 1:2],
                                          eyesb[0:32, 0:32]))
    c.add("vector", lambda e: e.tensor_copy(trow, psB[0:1, 32:64]))
    c.add("vector", lambda e: e.tensor_reduce(gmrow[0:1, 6:7], trow[0:1, 0:16],
                                              AX.X, ALU.add, False))
    c.add("vector", lambda e: e.tensor_reduce(gmrow[0:1, 7:8], trow[0:1, 16:32],
                                              AX.X, ALU.add, False))
    c.dma("gpsimd", out=ts_i[0:1, 0:2], in_=gmrow[0:1, 6:8])
    c.coll("AllReduce", ALU.add, [[0, 1], [2, 3], [4, 5], [6, 7]], [ts_i], [ts_o])
    c.dma("sync", out=trow[0:1, 0:2], in_=ts_o[0:1, 0:2])
    c.add("vector", lambda e: e.reciprocal(trow[0:1, 2:4], trow[0:1, 0:2]))
    c.add("tensor", lambda e: e.matmul(psB[:, 2:4], onesrF, trow[0:1, 2:4],
                                       start=True, stop=True))
    c.add("vector", lambda e: e.tensor_copy(rzb[:, 0:2], psB[:, 2:4]))
    for k in range(2):
        c.add("vector", lambda e, k=k: e.tensor_scalar(
            wv[:, 16 * k:16 * k + 16], exw[:, 16 * k:16 * k + 16],
            rzb[:, k:k + 1], None, ALU.mult))
    for tt in range(16):
        for e8 in range(8):
            c.add("vector", lambda e, tt=tt, e8=e8: e.tensor_scalar(
                tmp1[:, 0:1], topi[:, tt:tt + 1], float(e8), None, ALU.is_equal))
            c.add("vector", lambda e, tt=tt: e.tensor_tensor(
                tmp1[:, 0:1], tmp1[:, 0:1], wv[:, tt:tt + 1], ALU.mult))
            c.add("vector", lambda e, tt=tt, e8=e8: e.tensor_scalar(
                tmp2[:, 0:1], topi[:, 16 + tt:16 + tt + 1], float(e8), None,
                ALU.is_equal))
            c.add("vector", lambda e, tt=tt, e8=e8: e.scalar_tensor_tensor(
                cesb[:, 16 * e8 + tt:16 * e8 + tt + 1], tmp2[:, 0:1],
                wv[:, 16 + tt:16 + tt + 1], tmp1[:, 0:1], ALU.mult, ALU.add))

    trow16 = arR.al(512, parts=16)
    asl = arR.al(256)
    for tc in range(4):
        for i in range(16):
            c.dma("sync", out=xcol[:, 512 * i:512 * i + 512],
                  in_=x3T[128 * i:128 * i + 128, 512 * tc:512 * tc + 512])
        for s in range(16):
            e8, ug = s // 2, s % 2
            c.dma("sync", out=asl, in_=(I["upAB"] if ug == 0 else I["gaAB"])[e8])
            for i in range(16):
                c.add("tensor", lambda e, l=asl[:, 16 * i:16 * i + 16],
                      r=xcol[:, 512 * i:512 * i + 512], st=(i == 0), sp=(i == 15):
                      e.matmul(psA[0:16, 0:512], l, r, start=st, stop=sp))
            c.add("scalar", lambda e: e.activation(trow16, psA[0:16, 0:512],
                                                   AF.Copy, bias=0.0, scale=2.0))
            c.dma("gpsimd", out=tD[s][:, 512 * tc:512 * tc + 512], in_=trow16)
    ar.pop(); arR.pop()

    ar.push(); arR.push()
    xcol2 = arR.al(16 * 1024)
    wup = arR.al(2048)
    wga = arR.al(2048)
    tup = arR.al(1024, parts=16)
    tga = arR.al(1024, parts=16)
    ubB = arR.al(1024, parts=16)
    gbB = arR.al(1024, parts=16)
    dnA = arR.al(128)
    hRR = arR.al(1024)
    chR = arR.al(1024)
    Tacc = ar.al(3 * 1024)
    TaccR = arR.al(1024, parts=16)
    bu = ar.al(1024)
    bg = ar.al(1024)
    uu = ar.al(1024)
    gg = ar.al(1024)
    su = ar.al(1024)
    mixf = ar.al(1024)
    cbc = ar.al(8 * 1024)
    c8t = ar.al(128, parts=8)
    crow = ar.al(1024, parts=1)
    for tb2 in range(2):
        for i in range(16):
            c.dma("sync", out=xcol2[:, 1024 * i:1024 * i + 1024],
                  in_=x3T[128 * i:128 * i + 128, 1024 * tb2:1024 * tb2 + 1024])
        for e8 in range(8):
            c.add("tensor", lambda e, s=cesb[0:128, 16 * e8 + 8 * tb2:16 * e8 + 8 * tb2 + 8]:
                  e.transpose(psB[0:8, 0:128], s, eyesb))
            c.add("vector", lambda e: e.tensor_copy(c8t, psB[0:8, 0:128]))
            for q in range(8):
                c.dma("sync", out=crow[0:1, 128 * q:128 * q + 128],
                      in_=c8t[q:q + 1, 0:128])
            for cc in range(2):
                c.add("tensor", lambda e, cc=cc: e.matmul(
                    psB[:, 512 * cc:512 * cc + 512], onesrF,
                    crow[0:1, 512 * cc:512 * cc + 512], start=True, stop=True))
            c.add("scalar", lambda e, d=cbc[:, 1024 * e8:1024 * e8 + 1024]:
                  e.activation(d, psB[:, 0:1024], AF.Copy, bias=0.0, scale=1.0))
        c.add("vector", lambda e: e.memset(Tacc, 0.0))
        for hc in range(44):
            c.dma("sync", out=wup, in_=I["WupTB"][hc])
            c.dma("sync", out=wga, in_=I["WgTB"][hc])
            c.dma("sync", out=ubB, in_=I["upBB"][hc])
            c.dma("sync", out=gbB, in_=I["gaBB"][hc])
            c.dma("sync", out=dnA, in_=I["dnAB"][hc])
            c.add("vector", lambda e: e.memset(mixf, 0.0))
            for half in range(2):
                for i in range(16):
                    c.add("tensor", lambda e, d=psA[:, 512 * half:512 * half + 512],
                          l=wup[:, 128 * i:128 * i + 128],
                          r=xcol2[:, 1024 * i + 512 * half:1024 * i + 512 * half + 512],
                          st=(i == 0), sp=(i == 15): e.matmul(d, l, r, start=st, stop=sp))
                for i in range(16):
                    c.add("tensor", lambda e, d=psA[:, 1024 + 512 * half:1536 + 512 * half],
                          l=wga[:, 128 * i:128 * i + 128],
                          r=xcol2[:, 1024 * i + 512 * half:1024 * i + 512 * half + 512],
                          st=(i == 0), sp=(i == 15): e.matmul(d, l, r, start=st, stop=sp))
            c.add("scalar", lambda e: e.activation(bu, psA[:, 0:1024], AF.Copy,
                                                   bias=0.0, scale=1.0))
            c.add("scalar", lambda e: e.activation(bg, psA[:, 1024:2048], AF.Copy,
                                                   bias=0.0, scale=1.0))
            for e8 in range(8):
                c.dma("sync", out=tup, in_=tD[2 * e8][:, 1024 * tb2:1024 * tb2 + 1024])
                c.dma("sync", out=tga, in_=tD[2 * e8 + 1][:, 1024 * tb2:1024 * tb2 + 1024])
                for half in range(2):
                    c.add("tensor", lambda e, half=half, e8=e8: e.matmul(
                        psB[:, 512 * half:512 * half + 512],
                        ubB[:, 128 * e8:128 * e8 + 128],
                        tup[:, 512 * half:512 * half + 512], start=True, stop=True))
                c.add("vector", lambda e: e.tensor_tensor(uu, psB[:, 0:1024], bu,
                                                          ALU.add))
                c.add("scalar", lambda e: e.activation(su, uu, AF.Silu, bias=0.0,
                                                       scale=1.0))
                for half in range(2):
                    c.add("tensor", lambda e, half=half, e8=e8: e.matmul(
                        psB[:, 1024 + 512 * half:1536 + 512 * half],
                        gbB[:, 128 * e8:128 * e8 + 128],
                        tga[:, 512 * half:512 * half + 512], start=True, stop=True))
                c.add("vector", lambda e: e.tensor_tensor(gg, psB[:, 1024:2048], bg,
                                                          ALU.add))
                c.add("vector", lambda e: e.tensor_tensor(hRR, su, gg, ALU.mult))
                c.add("vector", lambda e, e8=e8: e.tensor_tensor(
                    chR, hRR.bitcast(F32), cbc[:, 1024 * e8:1024 * e8 + 1024],
                    ALU.mult))
                for half in range(2):
                    c.add("tensor", lambda e, half=half, e8=e8: e.matmul(
                        psB[0:16, 512 * half:512 * half + 512],
                        dnA[:, 16 * e8:16 * e8 + 16],
                        chR[:, 512 * half:512 * half + 512], start=True, stop=True))
                tsl = Tacc[32 * (e8 % 3):32 * (e8 % 3) + 16,
                           1024 * (e8 // 3):1024 * (e8 // 3) + 1024] if False else None
                base = 32 * (e8 % 3)
                cr = 1024 * (e8 // 3)
                c.add("vector", lambda e, d=Tacc[base:base + 16, cr:cr + 1024]:
                      e.tensor_tensor(d, d, psB[0:16, 0:1024], ALU.add))
                c.add("vector", lambda e: e.tensor_tensor(mixf, mixf,
                                                          chR.bitcast(F32), ALU.add))
            c.dma("gpsimd", out=mixD[128 * hc:128 * hc + 128,
                                     1024 * tb2:1024 * tb2 + 1024].bitcast(F32),
                  in_=mixf)
        for e8 in range(8):
            base = 32 * (e8 % 3)
            cr = 1024 * (e8 // 3)
            c.add("vector", lambda e, d=TaccR, s=Tacc[base:base + 16, cr:cr + 1024]:
                  e.tensor_scalar(d, s, 2.0, None, ALU.mult))
            c.dma("gpsimd", out=TaD[e8][:, 1024 * tb2:1024 * tb2 + 1024], in_=TaccR)
    ar.pop(); arR.pop()

    ar.push(); arR.push()
    mrhs = arR.al(1024)
    wdsl = arR.al(4 * 128)
    tasl = arR.al(8 * 1024, parts=16)
    dnB2 = arR.al(128, parts=16)
    oet4 = ar.al(4 * 1024)
    oev2 = ar.al(512)
    for tb2 in range(2):
        for e8 in range(8):
            c.dma("sync", out=tasl[:, 1024 * e8:1024 * e8 + 1024],
                  in_=TaD[e8][:, 1024 * tb2:1024 * tb2 + 1024])
        for eg in range(4):
            for hc in range(44):
                c.dma("sync", out=mrhs, in_=mixD[128 * hc:128 * hc + 128,
                                                 1024 * tb2:1024 * tb2 + 1024])
                for i in range(4):
                    ec = 4 * eg + i
                    c.dma("sync", out=wdsl[:, 128 * i:128 * i + 128],
                          in_=I["WdTB"][ec][:, 128 * hc:128 * hc + 128])
                for i in range(4):
                    for half in range(2):
                        c.add("tensor", lambda e, i=i, half=half, hc=hc: e.matmul(
                            psA[:, 0:512] if False else
                            (psA if i < 2 else psB)[:, 1024 * (i % 2) + 512 * half:
                                                    1024 * (i % 2) + 512 * half + 512],
                            wdsl[:, 128 * i:128 * i + 128],
                            mrhs[:, 512 * half:512 * half + 512],
                            start=(hc == 0), stop=False))
            for i in range(4):
                ec = 4 * eg + i
                for e8 in range(8):
                    c.dma("sync", out=dnB2, in_=I["dnBT"][e8][:, 128 * ec:128 * ec + 128])
                    for half in range(2):
                        c.add("tensor", lambda e, i=i, half=half, e8=e8, sp=(e8 == 7):
                              e.matmul((psA if i < 2 else psB)
                                       [:, 1024 * (i % 2) + 512 * half:
                                        1024 * (i % 2) + 512 * half + 512],
                                       dnB2,
                                       tasl[:, 1024 * e8 + 512 * half:
                                            1024 * e8 + 512 * half + 512],
                                       start=False, stop=sp))
            for i in range(4):
                pst = psA if i < 2 else psB
                c.add("scalar", lambda e, d=oet4[:, 1024 * i:1024 * i + 1024],
                      s=pst[:, 1024 * (i % 2):1024 * (i % 2) + 1024]:
                      e.activation(d, s, AF.Copy, bias=0.0, scale=1.0))
            for i in range(4):
                ec = 4 * eg + i
                for q in range(8):
                    pst2 = psA if q % 8 < 4 else psB
                    c.add("tensor", lambda e, q=q, pst2=pst2,
                          s=oet4[:, 1024 * i + 128 * q:1024 * i + 128 * q + 128],
                          ey=eyesb:
                          e.transpose(pst2[:, 128 * (q % 4):128 * (q % 4) + 128], s, ey))
                    if q % 4 == 3:
                        c.add("scalar", lambda e, pst2=pst2, d=oev2:
                              e.activation(d, pst2[:, 0:512], AF.Copy,
                                           bias=0.0, scale=1.0))
                        for w in range(4):
                            tq = q - 3 + w
                            c.dma("gpsimd",
                                  out=o2D[1024 * tb2 + 128 * tq:1024 * tb2 + 128 * tq + 128,
                                          128 * ec:128 * ec + 128],
                                  in_=oev2[:, 128 * w:128 * w + 128])
    ar.pop(); arR.pop()
    ln_pass(ctx, o2D, x3D, out_own)


def kernel(**ins):
    maps = [host_prep(ins, core) for core in range(8)]
    nc = build(stages=("A", "B", "C", "D"))
    res = run_bass_kernel_spmd(nc, maps, core_ids=list(range(8)))
    out = np.zeros((B, T, E), np.float32)
    for core in range(8):
        b, par = core // 2, core % 2
        oo = np.asarray(res.results[core]["out_own"])
        for j in range(NOT_):
            out[b, 128 * (2 * j + par):128 * (2 * j + par) + 128] = oo[128 * j:128 * j + 128]
    return out


# version 07
# speedup vs baseline: 1.5099x; 1.5099x over previous
import numpy as np
import contextlib
import concourse.bass as bass
import concourse.mybir as mybir
from concourse.bass_utils import run_bass_kernel_spmd

F32 = mybir.dt.float32
F32R = mybir.dt.float32r
AF = mybir.ActivationFunctionType
ALU = mybir.AluOpType
AX = mybir.AxisListType

MM_DT = F32R

B, T, E = 4, 4096, 2048
KV, G, D = 8, 4, 64
NOT_ = 16
TOK = NOT_ * 128

NH, P, N, CS = 64, 64, 128, 256
DIN = NH * P
CONVD = DIN + 2 * N
NHh = NH // 2
ZH = NHh * P
PRJL = ZH + ZH + 2 * N + NHh
XBCL = ZH + 2 * N

NE, RK, HID = 8, 16, 5632
ALPHA = 2.0


def host_prep(ins, core):
    b, par = core // 2, core % 2
    x = np.asarray(ins["L0_attn_x"][b])
    own_rows = np.concatenate([np.arange(128 * (2 * j + par), 128 * (2 * j + par) + 128)
                               for j in range(NOT_)])
    m = {}
    m["x_full"] = np.ascontiguousarray(x)
    m["x_own"] = np.ascontiguousarray(x[own_rows])

    WqT = ins["L0_attn_Wq"].T
    m["WqTB"] = np.ascontiguousarray(
        WqT.reshape(16, 128, 16, 128).transpose(2, 1, 0, 3).reshape(16, 128, 2048))
    m["WkT"] = np.ascontiguousarray(ins["L0_attn_Wk"].T)
    m["WvT"] = np.ascontiguousarray(ins["L0_attn_Wv"].T)
    WoT = ins["L0_attn_Wo"].T
    m["WoTB"] = np.ascontiguousarray(
        WoT.reshape(16, 128, 4, 512).transpose(2, 1, 0, 3).reshape(4, 128, 8192))
    bq = np.asarray(ins["L0_attn_bq"])
    m["bq"] = np.ascontiguousarray(bq.reshape(16, 128).T)
    idx = np.arange(E)
    shp = (idx // 64) * 64 + ((idx % 64) + 32) % 64
    m["bqs"] = np.ascontiguousarray(bq[shp].reshape(16, 128).T)
    bk = np.asarray(ins["L0_attn_bk"])
    m["bk"] = np.ascontiguousarray(bk.reshape(4, 128).T)
    i5 = np.arange(512)
    shk = (i5 // 64) * 64 + ((i5 % 64) + 32) % 64
    m["bks"] = np.ascontiguousarray(bk[shk].reshape(4, 128).T)
    m["bv"] = np.ascontiguousarray(np.asarray(ins["L0_attn_bv"]).reshape(1, 512))
    m["bo"] = np.ascontiguousarray(np.asarray(ins["L0_attn_bo"]).reshape(1, 2048))

    j32 = np.arange(32, dtype=np.float64)
    theta = 10000.0 ** (-2.0 * j32 / 64.0)
    posf = np.arange(T, dtype=np.float64) + 1.0
    ang = posf[None, :] * np.concatenate([theta, theta])[:, None]
    cos64 = np.cos(ang)
    sin64 = np.sin(ang)
    sineff = np.concatenate([-sin64[:32], sin64[32:]], axis=0)
    cosk = np.tile(cos64, (2, 1)).astype(np.float32)
    sink = np.tile(sineff, (2, 1)).astype(np.float32)
    m["cosk"] = np.ascontiguousarray(cosk)
    m["sink"] = np.ascontiguousarray(sink)
    m["cosq"] = np.ascontiguousarray(cosk[:, own_rows])
    m["sinq"] = np.ascontiguousarray(sink[:, own_rows])

    msk = np.zeros((8, 128, 512), np.float32)
    tril_strict = np.tril(np.ones((128, 128), np.float32), -1)
    for r in range(8):
        for u in range(4):
            qb = par + 2 * u
            if r == qb:
                msk[r][:, 128 * u:128 * u + 128] += tril_strict * (-1e30)
            elif r > qb:
                msk[r][:, 128 * u:128 * u + 128] += -1e30
    m["amask"] = np.ascontiguousarray(msk.transpose(1, 0, 2).reshape(128, 8 * 512))
    m["eye"] = np.eye(128, dtype=np.float32)

    m["WfT"] = np.ascontiguousarray(ins["L0_ffn_weight"].T)
    m["gamma"] = np.ascontiguousarray(np.asarray(ins["L0_ffn_norm_weight"]).reshape(1, 2048))

    h0 = par * NHh
    W_in = np.asarray(ins["L1_attn_W_in"])
    Wl = np.concatenate([W_in[h0 * P:h0 * P + ZH],
                         W_in[DIN + h0 * P:DIN + h0 * P + ZH],
                         W_in[2 * DIN:2 * DIN + 2 * N],
                         W_in[DIN + CONVD + h0:DIN + CONVD + h0 + NHh]], axis=0)
    WinT = Wl.T
    m["WinTB"] = np.ascontiguousarray(
        np.pad(WinT, ((0, 0), (0, 4480 - PRJL))).reshape(16, 128, 35, 128)
        .transpose(2, 1, 0, 3).reshape(35, 128, 2048))
    cw = np.asarray(ins["L1_attn_conv_w"])[:, 0, :]
    cwl = np.concatenate([cw[h0 * P:h0 * P + ZH], cw[DIN:]], axis=0)
    m["convw"] = np.ascontiguousarray(cwl.reshape(18, 128, 4).transpose(1, 0, 2).reshape(128, 72))
    cb = np.asarray(ins["L1_attn_conv_b"])
    cbl = np.concatenate([cb[h0 * P:h0 * P + ZH], cb[DIN:]], axis=0)
    m["convb"] = np.ascontiguousarray(cbl.reshape(18, 128).T)
    m["dtb"] = np.ascontiguousarray(np.asarray(ins["L1_attn_dt_bias"][h0:h0 + NHh]).reshape(NHh, 1))
    m["Alog"] = np.ascontiguousarray(np.asarray(ins["L1_attn_A_log"][h0:h0 + NHh]).reshape(NHh, 1))
    m["Drep"] = np.ascontiguousarray(
        np.repeat(np.asarray(ins["L1_attn_D_param"][h0:h0 + NHh]), P).reshape(32, 64).T)
    m["nw"] = np.ascontiguousarray(
        np.asarray(ins["L1_attn_norm_w"][h0 * P:h0 * P + ZH]).reshape(16, 128).T)
    WoutT = np.asarray(ins["L1_attn_W_out"])[:, h0 * P:h0 * P + ZH].T
    m["WoutTB"] = np.ascontiguousarray(
        WoutT.reshape(16, 128, 4, 512).transpose(2, 1, 0, 3).reshape(4, 128, 8192))
    pat = np.zeros((2, 128, 256), np.float32)
    for po in range(2):
        s = np.arange(128)[:, None] + 128 * po
        l = np.arange(256)[None, :]
        pat[po] = (s <= l).astype(np.float32)
    m["triu"] = np.ascontiguousarray(pat.transpose(1, 0, 2).reshape(128, 512))

    WupT = np.asarray(ins["L1_ffn_w_up"]).T
    WgT = np.asarray(ins["L1_ffn_w_gate"]).T
    m["WupTB"] = np.ascontiguousarray(
        WupT.reshape(16, 128, 44, 128).transpose(2, 1, 0, 3).reshape(44, 128, 2048))
    m["WgTB"] = np.ascontiguousarray(
        WgT.reshape(16, 128, 44, 128).transpose(2, 1, 0, 3).reshape(44, 128, 2048))
    WdT = np.asarray(ins["L1_ffn_w_down"]).T
    m["WdTB"] = np.ascontiguousarray(
        WdT.reshape(44, 128, 16, 128).transpose(2, 1, 0, 3).reshape(16, 128, 5632))
    m["gwTB"] = np.ascontiguousarray(
        np.asarray(ins["L1_ffn_gate_w"]).T.reshape(16, 128, 8).transpose(1, 0, 2)
        .reshape(128, 128))
    m["upAB"] = np.ascontiguousarray(
        np.transpose(ins["L1_ffn_up_A"], (0, 2, 1)).reshape(8, 16, 128, 16)
        .transpose(0, 2, 1, 3).reshape(8, 128, 256))
    m["gaAB"] = np.ascontiguousarray(
        np.transpose(ins["L1_ffn_gate_A"], (0, 2, 1)).reshape(8, 16, 128, 16)
        .transpose(0, 2, 1, 3).reshape(8, 128, 256))
    upBT = np.transpose(ins["L1_ffn_up_B"], (0, 2, 1))
    gaBT = np.transpose(ins["L1_ffn_gate_B"], (0, 2, 1))
    m["upBB"] = np.ascontiguousarray(
        upBT.reshape(8, 16, 44, 128).transpose(2, 1, 0, 3).reshape(44, 16, 1024))
    m["gaBB"] = np.ascontiguousarray(
        gaBT.reshape(8, 16, 44, 128).transpose(2, 1, 0, 3).reshape(44, 16, 1024))
    dnAT = np.transpose(ins["L1_ffn_down_A"], (0, 2, 1))
    m["dnAB"] = np.ascontiguousarray(
        dnAT.reshape(8, 44, 128, 16).transpose(1, 2, 0, 3).reshape(44, 128, 128))
    m["dnBT"] = np.ascontiguousarray(np.transpose(ins["L1_ffn_down_B"], (0, 2, 1)))
    return m


class Chain:
    def __init__(self):
        self.ops = []
        self.solo = False

    def add(self, eng, fn, w=1):
        self.ops.append((eng, fn, w, self.solo))

    def dma(self, eng, **kw):
        self.add(eng, lambda e, kw=kw: e.dma_start(**kw), 16)

    def coll(self, kind, op, groups, ins, outs):
        self.add("gpsimd", lambda e: e.collective_compute(
            kind, op, replica_groups=groups, ins=ins, outs=outs), 1)

    def emit(self, nc, block, ch):
        n = len(self.ops)
        waits = []
        tot = 0
        for op in self.ops:
            waits.append(tot)
            tot += op[2]
        final = tot
        runs = []
        i = 0
        while i < n:
            eng = self.ops[i][0]
            isdma = self.ops[i][2] == 16
            j = i
            while j < n and self.ops[j][0] == eng and (self.ops[j][2] == 16) == isdma:
                j += 1
            runs.append((i, j, waits[i], isdma))
            i = j
        per_eng = {}
        for r in runs:
            per_eng.setdefault(self.ops[r[0]][0], []).append(r)

        def runner(eng_name):
            def body(e):
                for (i0, j0, wv, isdma) in per_eng.get(eng_name, []):
                    if isdma:
                        e.wait_ge(ch, wv)
                        for k in range(i0, j0):
                            self.ops[k][1](e).then_inc(ch, 16)
                    elif eng_name in ("tensor", "scalar"):
                        run_w = wv
                        k = i0
                        while k < j0:
                            if eng_name == "scalar" and self.ops[k][3]:
                                e.wait_ge(ch, run_w)
                                self.ops[k][1](e).then_inc(ch, self.ops[k][2])
                                run_w += self.ops[k][2]
                                k += 1
                                continue
                            k2 = k
                            cnt = 0
                            while (k2 < j0 and not
                                   (eng_name == "scalar" and self.ops[k2][3])):
                                cnt += self.ops[k2][2]
                                k2 += 1
                            e.wait_ge(ch, run_w)
                            last = None
                            for kk in range(k, k2):
                                last = self.ops[kk][1](e)
                            last.then_inc(ch, cnt)
                            run_w += cnt
                            k = k2
                    else:
                        run_w = wv
                        for k in range(i0, j0):
                            e.wait_ge(ch, run_w)
                            inst = self.ops[k][1](e)
                            if inst is None:
                                inst = e.nop()
                            inst.then_inc(ch, self.ops[k][2])
                            run_w += self.ops[k][2]
                if eng_name == "gpsimd":
                    e.wait_ge(ch, final)
            return body

        block.sync(runner("sync"))
        block.tensor(runner("tensor"))
        block.vector(runner("vector"))
        block.scalar(runner("scalar"))
        block.gpsimd(runner("gpsimd"))


class Arena:
    def __init__(self, sb):
        self.sb = sb
        self.off = 0
        self.marks = []

    def al(self, cols, parts=128):
        o = (self.off + 7) & ~7
        self.off = o + cols
        assert self.off <= self.sb.shape[1], f"SBUF arena overflow: {self.off}"
        return self.sb[0:parts, o:o + cols]

    def at(self, off, cols, parts=128):
        assert off + cols <= self.sb.shape[1]
        return self.sb[0:parts, off:off + cols]

    def push(self):
        self.marks.append(self.off)

    def pop(self):
        self.off = self.marks.pop()


def build(stages=("A", "B", "C", "D"), dbg=()):
    nc = bass.Bass("TRN2", target_bir_lowering=False)

    def din(name, shape, dt=F32):
        return nc.dram_tensor(name, shape, dt, kind="ExternalInput").ap()

    def dint(name, shape, dt=F32):
        return nc.dram_tensor(name, shape, dt, kind="Internal").ap()

    def dout(name, shape, dt=F32):
        return nc.dram_tensor(name, shape, dt, kind="ExternalOutput").ap()

    I = {}
    I["x_full"] = din("x_full", [T, E])
    I["x_own"] = din("x_own", [TOK, E])
    for nm, sh in [("WqTB", [16, 128, 2048]), ("WkT", [E, 512]), ("WvT", [E, 512]),
                   ("WoTB", [4, 128, 8192]), ("bv", [1, 512]), ("bo", [1, 2048])]:
        I[nm] = din(nm, sh, MM_DT)
    for nm, sh in [("bq", [128, 16]), ("bqs", [128, 16]), ("bk", [128, 4]),
                   ("bks", [128, 4]),
                   ("cosk", [128, T]), ("sink", [128, T]),
                   ("cosq", [128, TOK]), ("sinq", [128, TOK]),
                   ("amask", [128, 8 * 512]), ("eye", [128, 128]),
                   ("gamma", [1, 2048])]:
        I[nm] = din(nm, sh)
    I["WfT"] = din("WfT", [E, E])

    scr = {}
    scr["QT"] = dint("QT", [E, TOK], MM_DT)
    scr["KT"] = dint("KT", [512, T], MM_DT)
    scr["KTO"] = dint("KTO", [512, TOK], MM_DT)
    scr["dbgST"] = dint("dbgST", [128, 8])
    scr["dbgR"] = dint("dbgR", [128, 2048])
    scr["V8"] = dint("V8", [KV, 128, 2048], MM_DT)
    scr["yTB"] = dint("yTB", [NOT_, 128, 2048], MM_DT)
    scr["oD"] = dint("oD", [TOK, E])
    scr["x1D"] = dint("x1D", [TOK, E])
    scr["x2D"] = dint("x2D", [TOK, E])
    scr["x3D"] = dint("x3D", [TOK, E])
    scr["xnD"] = dint("xnD", [TOK, E])
    scr["xqTB"] = dint("xqTB", [NOT_, 128, 2048], MM_DT)
    scr["o2D"] = dint("o2D", [TOK, E])
    out_own = dout("out_own", [TOK, E])
    scr["out_own"] = out_own

    if "B" in stages:
        scr["mx_i"] = dint("mx_i", [1, 8])
        scr["mx_o"] = dint("mx_o", [1, 8])
    if "C" in stages:
        for nm, sh, dt in [("WinTB", [35, E, 128], MM_DT), ("convw", [128, 72], F32),
                           ("convb", [128, 18], F32), ("dtb", [NHh, 1], F32),
                           ("Alog", [NHh, 1], F32), ("Drep", [64, 32], F32),
                           ("nw", [128, 16], F32), ("WoutTB", [4, 128, 8192], MM_DT),
                           ("triu", [128, 512], F32)]:
            I[nm] = din(nm, sh, dt)
        scr["ag_i"] = dint("ag_i", [TOK, E])
        scr["ag_o"] = dint("ag_o", [2, TOK, E])
        scr["x2T"] = dint("x2T", [E, T], MM_DT)
        scr["prjD"] = dint("prjD", [4480, T])
        scr["gD"] = dint("gD", [ZH, T])
        scr["xcD"] = dint("xcD", [XBCL, T], MM_DT)
        scr["yMD"] = dint("yMD", [ZH, T], F32)
        scr["gnB"] = dint("gnB", [32, 128, 2048], MM_DT)
        scr["ss_i"] = dint("ss_i", [1, T])
        scr["ss_o"] = dint("ss_o", [1, T])
        scr["rs_i"] = dint("rs_i", [T, E])
        scr["rs_o"] = dint("rs_o", [TOK, E])
    if "D" in stages:
        for nm, sh in [("WupTB", [44, 128, 2048]), ("WgTB", [44, 128, 2048]),
                       ("WdTB", [16, 128, 5632]), ("gwTB", [128, 128]),
                       ("upAB", [NE, 128, 256]), ("gaAB", [NE, 128, 256]),
                       ("upBB", [44, 16, 1024]), ("gaBB", [44, 16, 1024]),
                       ("dnAB", [44, 128, 128]), ("dnBT", [NE, RK, E])]:
            I[nm] = din(nm, sh, MM_DT)
        scr["tD"] = dint("tD", [32, 16, TOK], MM_DT)
        scr["TaD"] = dint("TaD", [NE, 16, TOK], MM_DT)
        scr["x3T"] = dint("x3T", [E, TOK], MM_DT)
        scr["mixD"] = dint("mixD", [HID, TOK], MM_DT)
        scr["tm_i"] = dint("tm_i", [1, 8])
        scr["tm_o"] = dint("tm_o", [1, 8])
        scr["ts_i"] = dint("ts_i", [1, 8])
        scr["ts_o"] = dint("ts_o", [1, 8])
        scr["crD"] = dint("crD", [NE, 1, TOK])

    with contextlib.ExitStack() as stack:
        en = stack.enter_context
        en(nc.allow_low_precision(reason="float32r matmul operand tiles (4-byte fp32 storage)"))
        sbF = en(nc.sbuf_tensor([128, 21504], F32))
        sbR = en(nc.sbuf_tensor([128, 28672], F32R))
        ar = Arena(sbF)
        arR = Arena(sbR)
        idxu = en(nc.sbuf_tensor([128, 16], mybir.dt.uint32))
        psA = en(nc.psum_tensor([128, 2048], F32))
        psB = en(nc.psum_tensor([128, 2048], F32))
        ch = en(nc.semaphore("ch"))
        block = en(nc.Block())
        c = Chain()

        eyesb = ar.al(128)
        c.dma("sync", out=eyesb, in_=I["eye"])
        onesrF = ar.al(128, parts=1)
        c.add("vector", lambda e: e.memset(onesrF, 1.0))
        onescF = ar.al(8)
        c.add("vector", lambda e: e.memset(onescF[:, 0:1], 1.0))
        onesr = arR.al(128, parts=1)
        c.add("scalar", lambda e: e.activation(onesr, onesrF, AF.Copy, bias=0.0, scale=1.0))
        onesc = arR.al(8)
        c.add("scalar", lambda e: e.activation(onesc[:, 0:1], onescF[:, 0:1], AF.Copy,
                                               bias=0.0, scale=1.0))
        epscol = ar.al(8)
        c.add("vector", lambda e: e.memset(epscol[:, 0:1], 1e-5))

        ctx = dict(I=I, scr=scr, ar=ar, arR=arR, c=c, psA=psA, psB=psB, eyesb=eyesb,
                   onesr=onesr, onesc=onesc, onesrF=onesrF, onescF=onescF,
                   epscol=epscol, nc=nc, idxu=idxu)

        if "A" in stages:
            stage_A(ctx)
        if "B" in stages:
            stage_B(ctx)
        if "C" in stages:
            stage_C(ctx)
        if "D" in stages:
            stage_D(ctx)

        import os as _os
        _tr = _os.environ.get("K_TRUNC")
        if _tr:
            c.ops = c.ops[:int(_tr)]
        for name in dbg:
            src = scr[name]
            o = dout("dbg_" + name, list(src.shape))
            c.dma("gpsimd", out=o, in_=src)

        c.emit(nc, block, ch)
    return nc


def stage_A(ctx):
    I, scr, ar, arR, c = ctx["I"], ctx["scr"], ctx["ar"], ctx["arR"], ctx["c"]
    _scr = scr
    psA, psB = ctx["psA"], ctx["psB"]
    eyesb, onesr, onesc, epscol = ctx["eyesb"], ctx["onesr"], ctx["onesc"], ctx["epscol"]
    onesrF = ctx["onesrF"]
    onescF = ctx["onescF"]
    QT, KT, V8, yTB = scr["QT"], scr["KT"], scr["V8"], scr["yTB"]
    KTO = scr["KTO"]
    oD, x1D = scr["oD"], scr["x1D"]

    def rope_evac(ps, bcol, bscol, ctab, stab, out, ev, shf, t2):
        c.add("vector", lambda e: e.tensor_scalar(ev, ps, bcol, None, ALU.add))
        c.add("vector", lambda e: e.tensor_tensor(ev, ev, ctab, ALU.mult))
        for g0 in (0, 64):
            c.add("scalar", lambda e, g0=g0: e.activation(
                shf[g0:g0 + 32, :], ps[g0 + 32:g0 + 64, :], AF.Copy, bias=0.0, scale=1.0))
            c.add("scalar", lambda e, g0=g0: e.activation(
                shf[g0 + 32:g0 + 64, :], ps[g0:g0 + 32, :], AF.Copy, bias=0.0, scale=1.0))
        c.add("vector", lambda e: e.tensor_scalar(t2, shf, bscol, None, ALU.add))
        c.add("vector", lambda e: e.tensor_tensor(t2, t2, stab, ALU.mult))
        c.add("vector", lambda e: e.tensor_tensor(out, ev, t2, ALU.add))

    ar.push(); arR.push()
    xT = arR.al(16 * 512)
    ko = arR.al(512)
    evR = arR.al(512)
    bvsb = arR.al(512, parts=1)
    WkR = arR.al(16 * 512)
    WvR = arR.al(16 * 512)
    for ec in range(16):
        c.dma("sync", out=WkR[:, 512 * ec:512 * ec + 512],
              in_=I["WkT"][128 * ec:128 * ec + 128, :])
        c.dma("sync", out=WvR[:, 512 * ec:512 * ec + 512],
              in_=I["WvT"][128 * ec:128 * ec + 128, :])
    bksb = ar.al(8)
    bkssb = ar.al(8)
    c.dma("sync", out=bksb[:, 0:4], in_=I["bk"])
    c.dma("sync", out=bkssb[:, 0:4], in_=I["bks"])
    c.dma("sync", out=bvsb, in_=I["bv"])
    ckk = ar.al(4096)
    skk = ar.al(4096)
    c.dma("sync", out=ckk, in_=I["cosk"])
    c.dma("sync", out=skk, in_=I["sink"])
    xtb = ar.al(4 * 2048)
    ev = ar.al(512)
    shf = ar.al(512)
    t2 = ar.al(512)

    for tb in range(8):
        for u in range(4):
            c.dma("sync", out=xtb[:, 2048 * u:2048 * u + 2048],
                  in_=I["x_full"][512 * tb + 128 * u:512 * tb + 128 * u + 128, :])
        for ec in range(16):
            pst = psA if ec % 2 == 0 else psB
            for u in range(4):
                c.add("tensor", lambda e, d=pst[:, 128 * u:128 * u + 128],
                      s=xtb[:, 2048 * u + 128 * ec:2048 * u + 128 * ec + 128],
                      ey=eyesb: e.transpose(d, s, ey))
            c.add("scalar", lambda e, d=xT[:, 512 * ec:512 * ec + 512],
                  s=pst[:, 0:512]: e.activation(d, s, AF.Copy, bias=0.0, scale=1.0))
        for kt in range(4):
            for ec in range(16):
                c.add("tensor", lambda e, d=psA[:, 512 * kt:512 * kt + 512],
                      l=WkR[:, 512 * ec + 128 * kt:512 * ec + 128 * kt + 128],
                      r=xT[:, 512 * ec:512 * ec + 512],
                      st=(ec == 0), sp=(ec == 15): e.matmul(d, l, r, start=st, stop=sp))
        for kt in range(4):
            rope_evac(psA[:, 512 * kt:512 * kt + 512], bksb[:, kt:kt + 1],
                      bkssb[:, kt:kt + 1], ckk[:, 512 * tb:512 * tb + 512],
                      skk[:, 512 * tb:512 * tb + 512], ko, ev, shf, t2)
            c.dma("gpsimd", out=KT[128 * kt:128 * kt + 128, 512 * tb:512 * tb + 512],
                  in_=ko)
        for u in range(4):
            for ec in range(16):
                c.add("tensor", lambda e,
                      l=xT[:, 512 * ec + 128 * u:512 * ec + 128 * u + 128],
                      r=WvR[:, 512 * ec:512 * ec + 512],
                      st=(ec == 0): e.matmul(psB[:, 0:512], l, r, start=st, stop=False))
            c.add("tensor", lambda e: e.matmul(
                psB[:, 0:512], onesr, bvsb, start=False, stop=True))
            c.add("scalar", lambda e: e.activation(
                evR, psB[:, 0:512], AF.Copy, bias=0.0, scale=1.0))
            for h in range(8):
                c.dma("gpsimd",
                      out=V8[h][:, 64 * (4 * tb + u):64 * (4 * tb + u) + 64],
                      in_=evR[:, 64 * h:64 * h + 64])
    ar.pop(); arR.pop()

    ar.push(); arR.push()
    bqsb = ar.al(16)
    bqssb = ar.al(16)
    c.dma("sync", out=bqsb, in_=I["bq"])
    c.dma("sync", out=bqssb, in_=I["bqs"])
    cqq = ar.al(2048)
    sqq = ar.al(2048)
    c.dma("sync", out=cqq, in_=I["cosq"])
    c.dma("sync", out=sqq, in_=I["sinq"])
    bk2 = ar.al(8)
    bks2 = ar.al(8)
    c.dma("sync", out=bk2[:, 0:4], in_=I["bk"])
    c.dma("sync", out=bks2[:, 0:4], in_=I["bks"])
    xtb = ar.al(4 * 2048)
    ev = ar.al(512)
    shf = ar.al(512)
    t2 = ar.al(512)
    xT = arR.al(16 * 512)
    wq = arR.al(16 * 128)
    qo = arR.al(512)
    Wk2 = arR.al(16 * 512)
    for ec in range(16):
        c.dma("sync", out=Wk2[:, 512 * ec:512 * ec + 512],
              in_=I["WkT"][128 * ec:128 * ec + 128, :])
    for cb in range(4):
        for u in range(4):
            c.dma("sync", out=xtb[:, 2048 * u:2048 * u + 2048],
                  in_=I["x_own"][512 * cb + 128 * u:512 * cb + 128 * u + 128, :])
        for ec in range(16):
            pst = psA if ec % 2 == 0 else psB
            for u in range(4):
                c.add("tensor", lambda e, d=pst[:, 128 * u:128 * u + 128],
                      s=xtb[:, 2048 * u + 128 * ec:2048 * u + 128 * ec + 128],
                      ey=eyesb: e.transpose(d, s, ey))
            c.add("scalar", lambda e, d=xT[:, 512 * ec:512 * ec + 512],
                  s=pst[:, 0:512]: e.activation(d, s, AF.Copy, bias=0.0, scale=1.0))
        for qt in range(16):
            c.dma("sync", out=wq, in_=I["WqTB"][qt])
            for ec in range(16):
                c.add("tensor", lambda e, l=wq[:, 128 * ec:128 * ec + 128],
                      r=xT[:, 512 * ec:512 * ec + 512],
                      st=(ec == 0), sp=(ec == 15): e.matmul(psA[:, 0:512], l, r,
                                                           start=st, stop=sp))
            rope_evac(psA[:, 0:512], bqsb[:, qt:qt + 1], bqssb[:, qt:qt + 1],
                      cqq[:, 512 * cb:512 * cb + 512], sqq[:, 512 * cb:512 * cb + 512],
                      qo, ev, shf, t2)
            c.dma("gpsimd", out=QT[128 * qt:128 * qt + 128, 512 * cb:512 * cb + 512],
                  in_=qo)
        for kt in range(4):
            for ec in range(16):
                c.add("tensor", lambda e, d=psA[:, 512 * kt:512 * kt + 512],
                      l=Wk2[:, 512 * ec + 128 * kt:512 * ec + 128 * kt + 128],
                      r=xT[:, 512 * ec:512 * ec + 512],
                      st=(ec == 0), sp=(ec == 15): e.matmul(d, l, r, start=st, stop=sp))
        for kt in range(4):
            rope_evac(psA[:, 512 * kt:512 * kt + 512], bk2[:, kt:kt + 1],
                      bks2[:, kt:kt + 1], cqq[:, 512 * cb:512 * cb + 512],
                      sqq[:, 512 * cb:512 * cb + 512], qo, ev, shf, t2)
            c.dma("gpsimd", out=KTO[128 * kt:128 * kt + 128, 512 * cb:512 * cb + 512],
                  in_=qo)
    ar.pop(); arR.pop()

    ar.push(); arR.push()
    amsk = ar.al(8 * 512)
    c.dma("sync", out=amsk, in_=I["amask"])
    vplain = ar.al(2048)
    rbc_sb = ar.al(512, parts=64)
    onescv = ar.al(32)
    onesrow = ar.al(4096, parts=1)
    negC = ar.al(8, parts=1)
    prod = ar.al(2048, parts=64)
    zcol = ar.al(8)
    c.add("vector", lambda e: e.memset(onescv, 1.0))
    c.add("vector", lambda e: e.memset(onesrow, 1.0))
    c.add("vector", lambda e: e.memset(negC[0:1, 0:1], -200.0))
    c.add("vector", lambda e: e.memset(zcol[:, 0:1], 0.0))
    ksb = arR.al(4096, parts=65)
    v65 = arR.al(32 * 65)
    qsb = arR.al(2048, parts=65)
    kto = arR.al(2048, parts=64)
    PT = arR.al(32 * 512)
    ysb = arR.al(512, parts=64)
    rrow = arR.al(512, parts=1)
    c.add("scalar", lambda e, d=v65.rearrange("p (c w) -> p c w", w=65)[:, :, 64:65],
          s=onescv.rearrange("p (c w) -> p c w", w=1):
          e.activation(d, s, AF.Copy, bias=0.0, scale=1.0))
    c.add("scalar", lambda e, d=ksb[64:65, :], s=onesrow:
          e.activation(d, s, AF.Copy, bias=0.0, scale=1.0))

    for h in range(KV):
        c.dma("sync", out=ksb[0:64, :], in_=KT[64 * h:64 * h + 64, :])
        c.dma("sync", out=kto, in_=KTO[64 * h:64 * h + 64, :])
        c.dma("sync", out=vplain, in_=V8[h].bitcast(F32))
        c.add("vector", lambda e, d=v65.rearrange("p (c w) -> p c w", w=65)[:, :, 0:64],
              s=vplain.rearrange("p (c w) -> p c w", w=64): e.tensor_copy(d, s))
        for g in range(4):
            c.dma("sync", out=qsb[0:64, :], in_=QT[256 * h + 64 * g:256 * h + 64 * g + 64, :])
            c.add("vector", lambda e: e.tensor_tensor(
                prod, qsb[0:64, :].bitcast(F32), kto.bitcast(F32), ALU.mult))
            for cc in range(4):
                c.add("tensor", lambda e, cc=cc: e.matmul(
                    psB[0:1, 512 * cc:512 * cc + 512], onescF[0:64, 0:1],
                    prod[:, 512 * cc:512 * cc + 512], start=True, stop=True))
            c.add("vector", lambda e: e.tensor_scalar(
                qsb[64:65, :], psB[0:1, 0:2048], -1.0, -200.0, ALU.mult, ALU.add))
            for sp in range(4):
                L = 8 * sp + 8
                qc = 512 * sp
                ng = L // 4
                for gr in range(ng):
                    for r4 in range(4):
                        kc = 4 * gr + r4
                        c.add("tensor", lambda e, kc=kc, r4=r4, qc=qc: e.matmul(
                            psA[:, 512 * r4:512 * r4 + 512],
                            ksb[:, 128 * kc:128 * kc + 128],
                            qsb[:, qc:qc + 512], start=True, stop=True))
                    base = 4 * gr - 8 * sp
                    if base >= 0:
                        c.add("vector", lambda e, base=base: e.tensor_tensor(
                            psA[:, 0:2048], psA[:, 0:2048],
                            amsk[:, 512 * base:512 * base + 2048], ALU.add))
                    c.add("scalar", lambda e, gr=gr: e.activation(
                        PT[:, 2048 * gr:2048 * gr + 2048], psA[:, 0:2048], AF.Exp,
                        bias=zcol[:, 0:1], scale=0.125))
                for kc in range(L):
                    c.add("tensor", lambda e, kc=kc, L=L: e.matmul(
                        psB[0:65, 0:512], v65[:, 65 * kc:65 * kc + 65],
                        PT[:, 512 * kc:512 * kc + 512],
                        start=(kc == 0), stop=(kc == L - 1)))
                c.add("vector", lambda e: e.reciprocal(rrow, psB[64:65, 0:512]))
                c.add("tensor", lambda e: e.matmul(psB[0:64, 512:1024], onesr[0:1, 0:64],
                                                   rrow, start=True, stop=True))
                c.add("scalar", lambda e: e.activation(rbc_sb, psB[0:64, 512:1024],
                                                       AF.Copy, bias=0.0, scale=1.0))
                c.add("vector", lambda e: e.tensor_tensor(ysb, psB[0:64, 0:512],
                                                          rbc_sb, ALU.mult))
                q0 = 4 * h + g
                for w in range(4):
                    bi = 4 * sp + w
                    c.dma("gpsimd",
                          out=yTB[bi][64 * (q0 % 2):64 * (q0 % 2) + 64,
                                      128 * (q0 // 2):128 * (q0 // 2) + 128],
                          in_=ysb[:, 128 * w:128 * w + 128])
    ar.pop(); arR.pop()

    ar.push(); arR.push()
    wo = arR.al(16 * 512)
    ytb = arR.al(16 * 128)
    bosb = arR.al(2048, parts=1)
    c.dma("sync", out=bosb, in_=I["bo"])
    ev = ar.al(512)
    for ec in range(4):
        c.dma("sync", out=wo, in_=I["WoTB"][ec])
        for tt in range(16):
            c.dma("sync", out=ytb, in_=yTB[tt])
            for fc in range(16):
                c.add("tensor", lambda e, fc=fc: e.matmul(
                    psA[:, 0:512], ytb[:, 128 * fc:128 * fc + 128],
                    wo[:, 512 * fc:512 * fc + 512],
                    start=(fc == 0), stop=False))
            c.add("tensor", lambda e, ec=ec: e.matmul(
                psA[:, 0:512], onesr, bosb[0:1, 512 * ec:512 * ec + 512],
                start=False, stop=True))
            c.add("scalar", lambda e: e.activation(ev, psA[:, 0:512], AF.Copy,
                                                   bias=0.0, scale=1.0))
            c.dma("gpsimd", out=oD[128 * tt:128 * tt + 128, 512 * ec:512 * ec + 512], in_=ev)
    ar.pop(); arR.pop()

    ln_pass(ctx, oD, I["x_own"], x1D)


def sbRbase(ar):
    return 0


def ln_pass(ctx, aD, bD, outD):
    ar, c = ctx["ar"], ctx["c"]
    epscol = ctx["epscol"]
    ar.push()
    ta = ar.al(2048)
    tb = ar.al(2048)
    r = ar.al(2048)
    sq = ar.al(2048)
    st = ar.al(8)
    for tt in range(16):
        c.dma("sync", out=ta, in_=aD[128 * tt:128 * tt + 128, :])
        c.dma("sync", out=tb, in_=bD[128 * tt:128 * tt + 128, :])
        c.add("vector", lambda e: e.tensor_tensor(r, ta, tb, ALU.add))
        c.add("vector", lambda e: e.tensor_reduce(st[:, 0:1], r, AX.X, ALU.add, False))
        c.add("vector", lambda e: e.tensor_scalar(st[:, 1:2], st[:, 0:1],
                                                  1.0 / 2048.0, None, ALU.mult))
        c.add("scalar", lambda e: e.activation(sq, r, AF.Square, bias=0.0, scale=1.0))
        c.add("vector", lambda e: e.tensor_reduce(st[:, 2:3], sq, AX.X, ALU.add, False))
        c.add("vector", lambda e: e.tensor_scalar(st[:, 3:4], st[:, 2:3],
                                                  1.0 / 2048.0, None, ALU.mult))
        c.add("vector", lambda e: e.tensor_tensor(st[:, 4:5], st[:, 1:2],
                                                  st[:, 1:2], ALU.mult))
        c.add("vector", lambda e: e.tensor_tensor(st[:, 5:6], st[:, 3:4],
                                                  st[:, 4:5], ALU.subtract))
        c.add("scalar", lambda e: e.activation(st[:, 6:7], st[:, 5:6], AF.Sqrt,
                                               bias=epscol[:, 0:1], scale=1.0))
        c.add("vector", lambda e: e.reciprocal(st[:, 7:8], st[:, 6:7]))
        c.add("vector", lambda e: e.tensor_scalar(ta, r, st[:, 1:2], st[:, 7:8],
                                                  ALU.subtract, ALU.mult))
        if tt == 0:
            c.dma("gpsimd", out=ctx["scr"]["dbgST"], in_=st)
            c.dma("gpsimd", out=ctx["scr"]["dbgR"], in_=r)
        c.dma("gpsimd", out=outD[128 * tt:128 * tt + 128, :], in_=ta)
    ar.pop()


def stage_B(ctx):
    I, scr, ar, arR, c = ctx["I"], ctx["scr"], ctx["ar"], ctx["arR"], ctx["c"]
    psA, psB = ctx["psA"], ctx["psB"]
    eyesb, onesr, onesrF, onescF = ctx["eyesb"], ctx["onesr"], ctx["onesrF"], ctx["onescF"]
    x1D, xnD, xqTB, o2D, x2D = scr["x1D"], scr["xnD"], scr["xqTB"], scr["o2D"], scr["x2D"]
    mx_i, mx_o = scr["mx_i"], scr["mx_o"]

    ar.push(); arR.push()
    wt = ar.al(2048)
    wsum = ar.al(8)
    sc = ar.al(8, parts=1)
    for ec in range(16):
        c.dma("sync", out=wt, in_=I["WfT"][128 * ec:128 * ec + 128, :])
        c.add("vector", lambda e, ec=ec: e.tensor_reduce(wsum[:, 1:2], wt, AX.X,
                                                         ALU.add, True))
        if ec == 0:
            c.add("vector", lambda e: e.tensor_copy(wsum[:, 0:1], wsum[:, 1:2]))
        else:
            c.add("vector", lambda e: e.tensor_tensor(wsum[:, 0:1], wsum[:, 0:1],
                                                      wsum[:, 1:2], ALU.add))
    c.add("tensor", lambda e: e.matmul(psB[0:1, 0:1], wsum[:, 0:1], onescF[:, 0:1],
                                       start=True, stop=True))
    c.add("vector", lambda e: e.tensor_scalar(sc[0:1, 0:1], psB[0:1, 0:1],
                                              1.0 / (2048.0 * 2048.0), 1e-5,
                                              ALU.mult, ALU.max))
    c.add("vector", lambda e: e.reciprocal(sc[0:1, 1:2], sc[0:1, 0:1]))

    gsb = ar.al(2048, parts=1)
    c.dma("sync", out=gsb, in_=I["gamma"])
    gbc = ar.al(2048)
    for cc in range(4):
        c.add("tensor", lambda e, cc=cc: e.matmul(
            psA[:, 512 * cc:512 * cc + 512], onesrF, gsb[0:1, 512 * cc:512 * cc + 512],
            start=True, stop=True))
    c.add("scalar", lambda e: e.activation(gbc, psA[:, 0:2048], AF.Copy, bias=0.0, scale=1.0))
    xt = ar.al(2048)
    sq = ar.al(2048)
    st = ar.al(8)
    gmax = ar.al(8)
    eps6 = ar.al(8)
    c.add("vector", lambda e: e.memset(eps6[:, 0:1], 1e-6))
    xn = ar.al(2048)
    for tt in range(16):
        c.dma("sync", out=xt, in_=x1D[128 * tt:128 * tt + 128, :])
        c.add("scalar", lambda e: e.activation(sq, xt, AF.Square, bias=0.0, scale=1.0))
        c.add("vector", lambda e: e.tensor_reduce(st[:, 0:1], sq, AX.X, ALU.add, False))
        c.add("vector", lambda e: e.tensor_scalar(st[:, 1:2], st[:, 0:1],
                                                  1.0 / 2048.0, None, ALU.mult))
        c.add("scalar", lambda e: e.activation(st[:, 2:3], st[:, 1:2], AF.Sqrt,
                                               bias=eps6[:, 0:1], scale=1.0))
        c.add("vector", lambda e: e.reciprocal(st[:, 3:4], st[:, 2:3]))
        c.add("vector", lambda e: e.scalar_tensor_tensor(xn, xt, st[:, 3:4], gbc,
                                                         ALU.mult, ALU.mult))
        c.dma("gpsimd", out=xnD[128 * tt:128 * tt + 128, :], in_=xn)
        c.add("vector", lambda e: e.tensor_reduce(st[:, 4:5], xn, AX.X, ALU.max, True))
        if tt == 0:
            c.add("vector", lambda e: e.tensor_copy(gmax[:, 0:1], st[:, 4:5]))
        else:
            c.add("vector", lambda e: e.tensor_tensor(gmax[:, 0:1], gmax[:, 0:1],
                                                      st[:, 4:5], ALU.max))
    c.add("tensor", lambda e: e.transpose(psB[0:1, 0:128], gmax[:, 0:1], eyesb))
    c.add("vector", lambda e: e.tensor_reduce(sc[0:1, 2:3], psB[0:1, 0:128],
                                              AX.X, ALU.max, False))
    c.dma("gpsimd", out=mx_i[0:1, 0:1], in_=sc[0:1, 2:3])
    c.coll("AllReduce", ALU.max, [[0, 1, 2, 3, 4, 5, 6, 7]], [mx_i], [mx_o])
    c.dma("sync", out=sc[0:1, 3:4], in_=mx_o[0:1, 0:1])
    c.add("vector", lambda e: e.tensor_scalar(sc[0:1, 3:4], sc[0:1, 3:4], 1e-5,
                                              None, ALU.max))
    c.add("vector", lambda e: e.reciprocal(sc[0:1, 4:5], sc[0:1, 3:4]))
    c.add("vector", lambda e: e.tensor_scalar(sc[0:1, 4:5], sc[0:1, 4:5], 127.0,
                                              None, ALU.mult))
    c.add("vector", lambda e: e.tensor_tensor(sc[0:1, 5:6], sc[0:1, 3:4],
                                              sc[0:1, 0:1], ALU.mult))
    c.add("vector", lambda e: e.tensor_scalar(sc[0:1, 5:6], sc[0:1, 5:6],
                                              1.0 / 127.0, None, ALU.mult))
    kcol = ar.al(8)
    occol = ar.al(8)
    iwcol = ar.al(8)
    c.add("tensor", lambda e: e.matmul(psB[:, 128:129], onesrF, sc[0:1, 4:5],
                                       start=True, stop=True))
    c.add("vector", lambda e: e.tensor_copy(kcol[:, 0:1], psB[:, 128:129]))
    c.add("tensor", lambda e: e.matmul(psB[:, 129:130], onesrF, sc[0:1, 5:6],
                                       start=True, stop=True))
    c.add("vector", lambda e: e.tensor_copy(occol[:, 0:1], psB[:, 129:130]))
    c.add("tensor", lambda e: e.matmul(psB[:, 130:131], onesrF, sc[0:1, 1:2],
                                       start=True, stop=True))
    c.add("vector", lambda e: e.tensor_copy(iwcol[:, 0:1], psB[:, 130:131]))

    q1 = ar.al(2048)
    rb = ar.al(2048)
    cb_ = ar.al(2048)
    xqR = arR.al(2048)
    for tt in range(16):
        c.dma("sync", out=xt, in_=xnD[128 * tt:128 * tt + 128, :])
        c.add("vector", lambda e: e.tensor_scalar(q1, xt, kcol[:, 0:1], None, ALU.mult))
        c.add("vector", lambda e: e.tensor_scalar(q1, q1, 127.0, -128.0, ALU.min, ALU.max))
        c.add("vector", lambda e: e.tensor_scalar(q1, q1, 128.5, None, ALU.add))
        c.add("vector", lambda e: e.memset(sq, 0.0))
        for b in [128.0, 64.0, 32.0, 16.0, 8.0, 4.0, 2.0, 1.0]:
            c.add("vector", lambda e, b=b: e.tensor_scalar(rb, sq, b, None, ALU.add))
            c.add("vector", lambda e: e.tensor_tensor(cb_, q1, rb, ALU.is_ge))
            c.add("vector", lambda e, b=b: e.scalar_tensor_tensor(sq, cb_, b, sq,
                                                                  ALU.mult, ALU.add))
        c.add("vector", lambda e: e.tensor_scalar(sq, sq, 128.0, None, ALU.subtract))
        for ec in range(16):
            g = ec // 4
            pst = psA if g % 2 == 0 else psB
            c.add("tensor", lambda e, d=pst[:, 128 * (ec % 4):128 * (ec % 4) + 128],
                  s=sq[:, 128 * ec:128 * ec + 128], ey=eyesb: e.transpose(d, s, ey))
            if ec % 4 == 3:
                c.add("scalar", lambda e, d=xqR[:, 512 * g:512 * g + 512],
                      s=pst[:, 0:512]: e.activation(d, s, AF.Copy, bias=0.0, scale=1.0))
        c.dma("gpsimd", out=xqTB[tt].bitcast(F32), in_=xqR.bitcast(F32))
    ar.pop(); arR.pop()

    ar.push(); arR.push()
    wt4 = ar.al(512)
    wn4 = ar.al(512)
    a4 = ar.al(512)
    b4 = ar.al(512)
    osb = ar.al(512)
    wqR = arR.al(16 * 512)
    xqt = arR.al(2048)
    iw2 = ar.al(8)
    oc2 = ar.al(8)
    c.add("vector", lambda e: e.tensor_copy(iw2[:, 0:1], iwcol[:, 0:1]))
    c.add("vector", lambda e: e.tensor_copy(oc2[:, 0:1], occol[:, 0:1]))
    for oc in range(4):
        for ec in range(16):
            c.dma("sync", out=wt4,
                  in_=I["WfT"][128 * ec:128 * ec + 128, 512 * oc:512 * oc + 512])
            c.add("vector", lambda e: e.tensor_scalar(wn4, wt4, iw2[:, 0:1], None,
                                                      ALU.mult))
            c.add("vector", lambda e: e.tensor_scalar(a4, wn4, 0.5, None, ALU.is_ge))
            c.add("vector", lambda e: e.tensor_scalar(b4, wn4, -0.5, None, ALU.is_le))
            c.add("vector", lambda e, d=wqR[:, 512 * ec:512 * ec + 512]:
                  e.tensor_tensor(d, a4, b4, ALU.subtract))
        for tt in range(16):
            c.dma("sync", out=xqt.bitcast(F32), in_=xqTB[tt].bitcast(F32))
            for ec in range(16):
                c.add("tensor", lambda e, l=xqt[:, 128 * ec:128 * ec + 128],
                      r=wqR[:, 512 * ec:512 * ec + 512],
                      st=(ec == 0), sp=(ec == 15): e.matmul(psA[:, 0:512], l, r,
                                                            start=st, stop=sp))
            c.add("scalar", lambda e: e.activation(osb, psA[:, 0:512], AF.Copy,
                                                   bias=0.0, scale=oc2[:, 0:1]))
            c.dma("gpsimd", out=o2D[128 * tt:128 * tt + 128, 512 * oc:512 * oc + 512],
                  in_=osb)
    ar.pop(); arR.pop()

    ln_pass(ctx, o2D, x1D, x2D)


def stage_C(ctx):
    I, scr, ar, arR, c = ctx["I"], ctx["scr"], ctx["ar"], ctx["arR"], ctx["c"]
    psA, psB = ctx["psA"], ctx["psB"]
    eyesb, onesr, onesrF, onescF = ctx["eyesb"], ctx["onesr"], ctx["onesrF"], ctx["onescF"]
    x2D, x3D = scr["x2D"], scr["x3D"]
    ag_i, ag_o, x2T, prjD = scr["ag_i"], scr["ag_o"], scr["x2T"], scr["prjD"]
    xcD, yMD, gD, gnB = scr["xcD"], scr["yMD"], scr["gD"], scr["gnB"]
    ss_i, ss_o, rs_i, rs_o = scr["ss_i"], scr["ss_o"], scr["rs_i"], scr["rs_o"]

    c.dma("sync", out=ag_i, in_=x2D)
    c.coll("AllGather", ALU.bypass, [[0, 1], [2, 3], [4, 5], [6, 7]], [ag_i], [ag_o])

    ar.push(); arR.push()
    xtl = ar.al(2048)
    xev = arR.al(512)
    for p in range(32):
        c.dma("sync", out=xtl, in_=ag_o[p % 2][128 * (p // 2):128 * (p // 2) + 128, :])
        for g4 in range(4):
            pst = psA if g4 % 2 == 0 else psB
            for i in range(4):
                ec = 4 * g4 + i
                c.add("tensor", lambda e, d=pst[:, 128 * i:128 * i + 128],
                      s=xtl[:, 128 * ec:128 * ec + 128], ey=eyesb: e.transpose(d, s, ey))
            c.add("scalar", lambda e, d=xev, s=pst[:, 0:512]:
                  e.activation(d, s, AF.Copy, bias=0.0, scale=1.0))
            for i in range(4):
                ec = 4 * g4 + i
                c.dma("gpsimd", out=x2T[128 * ec:128 * ec + 128, 128 * p:128 * p + 128],
                      in_=xev[:, 128 * i:128 * i + 128])
    ar.pop(); arR.pop()

    ar.push(); arR.push()
    xcol = arR.al(16 * 512)
    wsl0 = arR.al(2048)
    wsl1 = arR.al(2048)
    pev0 = ar.al(512)
    pev1 = ar.al(512)
    for tc in range(8):
        for i in range(16):
            c.dma("sync", out=xcol[:, 512 * i:512 * i + 512],
                  in_=x2T[128 * i:128 * i + 128, 512 * tc:512 * tc + 512])
        for pr0 in range(0, 35, 2):
            prs = [pr0] if pr0 + 1 >= 35 else [pr0, pr0 + 1]
            for j, pr in enumerate(prs):
                c.dma("sync", out=(wsl0 if j == 0 else wsl1), in_=I["WinTB"][pr])
            for j, pr in enumerate(prs):
                w = wsl0 if j == 0 else wsl1
                for i in range(16):
                    c.add("tensor", lambda e, d=psA[:, 512 * j:512 * j + 512],
                          l=w[:, 128 * i:128 * i + 128],
                          r=xcol[:, 512 * i:512 * i + 512],
                          st=(i == 0), sp=(i == 15): e.matmul(d, l, r,
                                                              start=st, stop=sp))
            for j, pr in enumerate(prs):
                c.add("scalar", lambda e, d=(pev0 if j == 0 else pev1),
                      s=psA[:, 512 * j:512 * j + 512]:
                      e.activation(d, s, AF.Copy, bias=0.0, scale=1.0))
            for j, pr in enumerate(prs):
                c.dma("gpsimd",
                      out=prjD[128 * pr:128 * pr + 128, 512 * tc:512 * tc + 512],
                      in_=(pev0 if j == 0 else pev1))
    ar.pop(); arR.pop()

    ar.push(); arR.push()
    cwsb = ar.al(72)
    cbsb = ar.al(18)
    c.dma("sync", out=cwsb, in_=I["convw"])
    c.dma("sync", out=cbsb, in_=I["convb"])
    xcv = ar.al(4104)
    acc = ar.al(4096)
    xsl = arR.al(4096)
    for ct in range(18):
        c.add("vector", lambda e: e.memset(xcv[:, 0:3], 0.0))
        c.dma("sync", out=xcv[:, 3:4099], in_=prjD[2048 + 128 * ct:2048 + 128 * ct + 128, :])
        c.add("vector", lambda e, ct=ct: e.tensor_scalar(
            acc, xcv[:, 0:4096], cwsb[:, 4 * ct:4 * ct + 1], cbsb[:, ct:ct + 1],
            ALU.mult, ALU.add))
        for j in range(1, 4):
            c.add("vector", lambda e, ct=ct, j=j: e.scalar_tensor_tensor(
                acc, xcv[:, j:j + 4096], cwsb[:, 4 * ct + j:4 * ct + j + 1], acc,
                ALU.mult, ALU.add))
        c.add("scalar", lambda e: e.activation(xsl, acc, AF.Silu, bias=0.0, scale=1.0))
        c.dma("gpsimd", out=xcD[128 * ct:128 * ct + 128, :], in_=xsl)
    ar.pop(); arR.pop()

    ar.push(); arR.push()
    trisb = ar.al(512)
    c.dma("sync", out=trisb, in_=I["triu"])
    col32 = ar.al(8, parts=32)
    Acol = ar.al(8, parts=32)
    dtp = ar.al(4096, parts=32)
    tm1 = ar.al(4096, parts=32)
    c.dma("sync", out=dtp, in_=prjD[4352:4384, :])
    c.dma("sync", out=col32[:, 0:1], in_=I["dtb"])
    c.dma("sync", out=col32[:, 1:2], in_=I["Alog"])
    c.solo = True
    c.add("vector", lambda e: e.tensor_scalar(dtp, dtp, col32[:, 0:1], None, ALU.add))
    c.add("scalar", lambda e: e.activation(tm1, dtp, AF.Abs, bias=0.0, scale=1.0))
    c.add("scalar", lambda e: e.activation(tm1, tm1, AF.Exp, bias=0.0, scale=-1.0))
    c.add("vector", lambda e: e.tensor_scalar(tm1, tm1, 1.0, None, ALU.add))
    c.add("scalar", lambda e: e.activation(tm1, tm1, AF.Ln, bias=0.0, scale=1.0))
    c.add("scalar", lambda e: e.activation(dtp, dtp, AF.Relu, bias=0.0, scale=1.0))
    c.add("vector", lambda e: e.tensor_tensor(dtp, dtp, tm1, ALU.add))
    c.add("scalar", lambda e: e.activation(Acol[:, 0:1], col32[:, 1:2], AF.Exp,
                                           bias=0.0, scale=1.0))
    c.add("vector", lambda e: e.tensor_scalar(Acol[:, 1:2], Acol[:, 0:1], -1.0,
                                              None, ALU.mult))
    c.add("vector", lambda e: e.tensor_scalar(tm1, dtp, Acol[:, 1:2], None, ALU.mult))
    c.solo = False
    AcumC = ar.al(256, parts=32)
    expAC = ar.al(256, parts=32)
    dec32 = ar.al(256, parts=32)
    dAT32 = ar.al(64)
    AcumT = ar.al(64)
    dtT = ar.al(64)
    decT = ar.al(64)
    dtdecT = ar.al(64)
    edlbc = ar.al(32)
    erow = ar.al(32, parts=1)
    arow = ar.al(256, parts=1)
    erowh = ar.al(256, parts=1)
    bT = arR.al(256)
    cTt = arR.al(256)
    btok = arR.al(256)
    cbm = ar.al(512)
    argm = ar.al(256)
    tl = ar.al(256)
    Mt = arR.al(512)
    xcvf = ar.al(256, parts=64)
    xdt = arR.al(64)
    xdtd = arR.al(64)
    eab = ar.al(256, parts=64)
    t1s = ar.al(256, parts=64)
    y2s = ar.al(256, parts=64)
    y3s = ar.al(256, parts=64)
    SS = arR.al(2048)
    SSf = ar.al(2048)
    drs = ar.al(32, parts=64)
    c.dma("sync", out=drs, in_=I["Drep"])
    c.add("vector", lambda e: e.memset(SSf, 0.0))
    c.add("scalar", lambda e: e.activation(SS, SSf, AF.Copy, bias=0.0, scale=1.0))
    for ci in range(16):
        for sc in range(2):
            c.add("tensor", lambda e, s=tm1[:, 256 * ci + 128 * sc:256 * ci + 128 * sc + 128],
                  ey=eyesb[0:32, 0:32]: e.transpose(psA[:, 0:32], s, ey))
            c.add("vector", lambda e, d=dAT32[:, 32 * sc:32 * sc + 32]:
                  e.tensor_copy(d, psA[:, 0:32]))
        for sc in range(2):
            c.add("tensor", lambda e, l=dAT32[:, 32 * sc:32 * sc + 32],
                  r=trisb[:, 256 * sc:256 * sc + 256], st=(sc == 0), sp=(sc == 1):
                  e.matmul(psB[0:32, 0:256], l, r, start=st, stop=sp))
        c.solo = True
        c.add("scalar", lambda e: e.activation(AcumC, psB[0:32, 0:256], AF.Copy,
                                               bias=0.0, scale=1.0))
        c.add("scalar", lambda e: e.activation(expAC, AcumC, AF.Exp, bias=0.0, scale=1.0))
        c.add("scalar", lambda e: e.activation(dec32, AcumC, AF.Exp,
                                               bias=AcumC[:, 255:256], scale=-1.0))
        c.add("scalar", lambda e: e.activation(col32[:, 2:3], AcumC[:, 255:256],
                                               AF.Exp, bias=0.0, scale=1.0))
        c.solo = False
        for sc in range(2):
            c.add("tensor", lambda e, s=AcumC[:, 128 * sc:128 * sc + 128],
                  ey=eyesb[0:32, 0:32]: e.transpose(psA[:, 0:32], s, ey))
            c.add("vector", lambda e, d=AcumT[:, 32 * sc:32 * sc + 32]:
                  e.tensor_copy(d, psA[:, 0:32]))
            c.add("tensor", lambda e,
                  s=dtp[:, 256 * ci + 128 * sc:256 * ci + 128 * sc + 128],
                  ey=eyesb[0:32, 0:32]: e.transpose(psA[:, 32:64], s, ey))
            c.add("vector", lambda e, d=dtT[:, 32 * sc:32 * sc + 32]:
                  e.tensor_copy(d, psA[:, 32:64]))
            c.add("tensor", lambda e, s=dec32[:, 128 * sc:128 * sc + 128],
                  ey=eyesb[0:32, 0:32]: e.transpose(psA[:, 64:96], s, ey))
            c.add("vector", lambda e, d=decT[:, 32 * sc:32 * sc + 32]:
                  e.tensor_copy(d, psA[:, 64:96]))
        c.add("vector", lambda e: e.tensor_tensor(dtdecT, dtT, decT, ALU.mult))
        c.add("tensor", lambda e: e.transpose(psA[0:1, 96:128], col32[:, 2:3],
                                              eyesb[0:32, 0:32]))
        c.add("vector", lambda e: e.tensor_copy(erow[0:1, 0:32], psA[0:1, 96:128]))
        c.add("tensor", lambda e: e.matmul(psA[:, 128:160], onesrF, erow,
                                           start=True, stop=True))
        c.add("vector", lambda e: e.tensor_copy(edlbc, psA[:, 128:160]))
        c.dma("sync", out=bT, in_=xcD[2048:2176, 256 * ci:256 * ci + 256])
        c.dma("sync", out=cTt, in_=xcD[2176:2304, 256 * ci:256 * ci + 256])
        for sc in range(2):
            c.add("tensor", lambda e, s=bT[:, 128 * sc:128 * sc + 128].bitcast(F32),
                  ey=eyesb: e.transpose(psA[:, 0:128], s, ey))
            c.add("scalar", lambda e, d=btok[:, 128 * sc:128 * sc + 128]:
                  e.activation(d, psA[:, 0:128], AF.Copy, bias=0.0, scale=1.0))
            c.add("tensor", lambda e, l=bT[:, 128 * sc:128 * sc + 128], r=cTt:
                  e.matmul(psA[:, 256:512], l, r, start=True, stop=True))
            c.add("vector", lambda e, d=cbm[:, 256 * sc:256 * sc + 256], sc=sc:
                  e.tensor_tensor(d, psA[:, 256:512],
                                  trisb[:, 256 * sc:256 * sc + 256], ALU.mult))
        for h in range(NHh):
            c.dma("sync", out=arow, in_=AcumC[h:h + 1, 0:256])
            c.dma("sync", out=erowh, in_=expAC[h:h + 1, 0:256])
            for sc in range(2):
                c.add("tensor", lambda e: e.matmul(
                    psB[:, 1536:1792], onesrF, arow, start=True, stop=True))
                c.add("vector", lambda e, col=AcumT[:, 32 * sc + h:32 * sc + h + 1]:
                      e.tensor_scalar(argm, psB[:, 1536:1792], col, 0.0,
                                      ALU.subtract, ALU.min))
                c.add("scalar", lambda e: e.activation(tl, argm, AF.Exp,
                                                       bias=0.0, scale=1.0))
                c.add("vector", lambda e, d=Mt[:, 256 * sc:256 * sc + 256],
                      s2=cbm[:, 256 * sc:256 * sc + 256]:
                      e.tensor_tensor(d, tl, s2, ALU.mult))
            c.dma("sync", out=xcvf,
                  in_=xcD[64 * h:64 * h + 64, 256 * ci:256 * ci + 256].bitcast(F32))
            for sc in range(2):
                c.add("tensor", lambda e, s=xcvf[:, 128 * sc:128 * sc + 128],
                      ey=eyesb[0:64, 0:64]: e.transpose(psA[:, 640:704], s, ey))
                c.add("vector", lambda e, col=dtT[:, 32 * sc + h:32 * sc + h + 1]:
                      e.tensor_scalar(xdt, psA[:, 640:704], col, None, ALU.mult))
                c.add("vector", lambda e, col=dtdecT[:, 32 * sc + h:32 * sc + h + 1]:
                      e.tensor_scalar(xdtd, psA[:, 640:704], col, None, ALU.mult))
                c.add("tensor", lambda e, l=xdt, r=Mt[:, 256 * sc:256 * sc + 256],
                      st=(sc == 0), sp=(sc == 1): e.matmul(psB[0:64, 0:256], l, r,
                                                           start=st, stop=sp))
                c.add("tensor", lambda e, l=btok[:, 128 * sc:128 * sc + 128], r=xdtd,
                      st=(sc == 0), sp=(sc == 1): e.matmul(psB[:, 512:576], l, r,
                                                           start=st, stop=sp))
            c.add("tensor", lambda e, l=SS[:, 64 * h:64 * h + 64], r=cTt:
                  e.matmul(psB[0:64, 1024:1280], l, r, start=True, stop=True))
            c.add("tensor", lambda e: e.matmul(
                psA[0:64, 1536:1792], onesrF[0:1, 0:64], erowh,
                start=True, stop=True))
            c.add("scalar", lambda e: e.activation(eab, psA[0:64, 1536:1792],
                                                   AF.Copy, bias=0.0, scale=1.0))
            c.add("vector", lambda e: e.tensor_tensor(t1s, psB[0:64, 1024:1280],
                                                      eab, ALU.mult))
            c.add("vector", lambda e, h=h: e.scalar_tensor_tensor(
                y2s, xcvf, drs[:, h:h + 1], t1s, ALU.mult, ALU.add))
            c.add("vector", lambda e: e.tensor_tensor(y3s, y2s, psB[0:64, 0:256],
                                                      ALU.add))
            c.dma("gpsimd", out=yMD[64 * h:64 * h + 64, 256 * ci:256 * ci + 256],
                  in_=y3s)
            c.add("vector", lambda e, h=h: e.scalar_tensor_tensor(
                SS[:, 64 * h:64 * h + 64], SS[:, 64 * h:64 * h + 64],
                edlbc[:, h:h + 1], psB[:, 512:576], ALU.mult, ALU.add))
    ar.pop(); arR.pop()

    ar.push(); arR.push()
    yt5 = ar.al(4096)
    zt5 = ar.al(4096)
    for ct in range(16):
        c.dma("sync", out=yt5, in_=yMD[128 * ct:128 * ct + 128, :])
        c.dma("sync", out=zt5, in_=prjD[128 * ct:128 * ct + 128, :])
        c.add("scalar", lambda e: e.activation(zt5, zt5, AF.Silu, bias=0.0, scale=1.0))
        c.add("vector", lambda e: e.tensor_tensor(yt5, yt5, zt5, ALU.mult))
        c.add("scalar", lambda e: e.activation(zt5, yt5, AF.Square, bias=0.0, scale=1.0))
        for cc in range(8):
            pst = psA if cc < 4 else psB
            c.add("tensor", lambda e, d=pst[0:1, 512 * (cc % 4):512 * (cc % 4) + 512],
                  s=zt5[:, 512 * cc:512 * cc + 512], st=(ct == 0), sp=(ct == 15):
                  e.matmul(d, onescF[:, 0:1], s, start=st, stop=sp))
        c.dma("gpsimd", out=gD[128 * ct:128 * ct + 128, :], in_=yt5)
    ssrow = ar.al(4096, parts=1)
    c.add("scalar", lambda e: e.activation(ssrow[0:1, 0:2048], psA[0:1, 0:2048],
                                           AF.Copy, bias=0.0, scale=1.0))
    c.add("scalar", lambda e: e.activation(ssrow[0:1, 2048:4096], psB[0:1, 0:2048],
                                           AF.Copy, bias=0.0, scale=1.0))
    c.dma("gpsimd", out=ss_i, in_=ssrow)
    c.coll("AllReduce", ALU.add, [[0, 1], [2, 3], [4, 5], [6, 7]], [ss_i], [ss_o])
    c.dma("sync", out=ssrow, in_=ss_o)
    c.add("vector", lambda e: e.tensor_scalar(ssrow, ssrow, 1.0 / 4096.0, 1e-5,
                                              ALU.mult, ALU.add))
    c.add("scalar", lambda e: e.activation(ssrow, ssrow, AF.Sqrt, bias=0.0, scale=1.0))
    c.add("vector", lambda e: e.reciprocal(ssrow, ssrow))
    rsbc = ar.al(4096)
    for cc in range(8):
        pst = psA if cc < 4 else psB
        c.add("tensor", lambda e, d=pst[:, 512 * (cc % 4):512 * (cc % 4) + 512],
              s=ssrow[0:1, 512 * cc:512 * cc + 512]: e.matmul(d, onesrF, s,
                                                              start=True, stop=True))
    c.add("scalar", lambda e: e.activation(rsbc[:, 0:2048], psA[:, 0:2048],
                                           AF.Copy, bias=0.0, scale=1.0))
    c.add("scalar", lambda e: e.activation(rsbc[:, 2048:4096], psB[:, 0:2048],
                                           AF.Copy, bias=0.0, scale=1.0))
    nwsb = ar.al(16)
    c.dma("sync", out=nwsb, in_=I["nw"])
    gnR = arR.al(4096)
    for ct in range(16):
        c.dma("sync", out=yt5, in_=gD[128 * ct:128 * ct + 128, :])
        c.add("vector", lambda e, ct=ct: e.scalar_tensor_tensor(
            gnR, yt5, nwsb[:, ct:ct + 1], rsbc, ALU.mult, ALU.mult))
        for tw in range(32):
            c.dma("gpsimd", out=gnB[tw][:, 128 * ct:128 * ct + 128],
                  in_=gnR[:, 128 * tw:128 * tw + 128])
    ar.pop(); arR.pop()
    ar.push(); arR.push()
    wo5 = arR.al(16 * 512)
    gblk0 = arR.al(2048)
    gblk1 = arR.al(2048)
    oev0 = ar.al(512)
    oev1 = ar.al(512)
    for ec in range(4):
        c.dma("sync", out=wo5, in_=I["WoutTB"][ec])
        for tw0 in range(0, 32, 2):
            c.dma("sync", out=gblk0, in_=gnB[tw0])
            c.dma("sync", out=gblk1, in_=gnB[tw0 + 1])
            for j in range(2):
                g = gblk0 if j == 0 else gblk1
                for i in range(16):
                    c.add("tensor", lambda e, d=psA[:, 512 * j:512 * j + 512],
                          l=g[:, 128 * i:128 * i + 128],
                          r=wo5[:, 512 * i:512 * i + 512],
                          st=(i == 0), sp=(i == 15): e.matmul(d, l, r,
                                                              start=st, stop=sp))
            for j in range(2):
                c.add("scalar", lambda e, d=(oev0 if j == 0 else oev1),
                      s=psA[:, 512 * j:512 * j + 512]:
                      e.activation(d, s, AF.Copy, bias=0.0, scale=1.0))
            for j in range(2):
                tw = tw0 + j
                slot = (tw % 2) * 16 + tw // 2
                c.dma("gpsimd",
                      out=rs_i[128 * slot:128 * slot + 128, 512 * ec:512 * ec + 512],
                      in_=(oev0 if j == 0 else oev1))
    ar.pop(); arR.pop()
    c.coll("ReduceScatter", ALU.add, [[0, 1], [2, 3], [4, 5], [6, 7]], [rs_i], [rs_o])
    ln_pass(ctx, rs_o, x2D, x3D)


def stage_D(ctx):
    I, scr, ar, arR, c = ctx["I"], ctx["scr"], ctx["ar"], ctx["arR"], ctx["c"]
    psA, psB = ctx["psA"], ctx["psB"]
    eyesb, onesr, onesrF, onescF = ctx["eyesb"], ctx["onesr"], ctx["onesrF"], ctx["onescF"]
    nc = ctx["nc"]
    x3D, o2D, out_own = scr["x3D"], scr["o2D"], scr["out_own"]
    x3T, mixD, tD, TaD = scr["x3T"], scr["mixD"], scr["tD"], scr["TaD"]
    tm_i, tm_o, ts_i, ts_o = scr["tm_i"], scr["tm_o"], scr["ts_i"], scr["ts_o"]

    ar.push(); arR.push()
    xtl = ar.al(2048)
    xev = arR.al(512)
    for p in range(16):
        c.dma("sync", out=xtl, in_=x3D[128 * p:128 * p + 128, :])
        for g4 in range(4):
            pst = psA if g4 % 2 == 0 else psB
            for i in range(4):
                ec = 4 * g4 + i
                c.add("tensor", lambda e, d=pst[:, 128 * i:128 * i + 128],
                      s=xtl[:, 128 * ec:128 * ec + 128], ey=eyesb: e.transpose(d, s, ey))
            c.add("scalar", lambda e, d=xev, s=pst[:, 0:512]:
                  e.activation(d, s, AF.Copy, bias=0.0, scale=1.0))
            for i in range(4):
                ec = 4 * g4 + i
                c.dma("gpsimd", out=x3T[128 * ec:128 * ec + 128, 128 * p:128 * p + 128],
                      in_=xev[:, 128 * i:128 * i + 128])
    ar.pop(); arR.pop()

    ar.push(); arR.push()
    gsb = arR.al(128)
    c.dma("sync", out=gsb, in_=I["gwTB"])
    xcol = arR.al(16 * 512)
    lgT = ar.al(2048, parts=8)
    lg = ar.al(128)
    topv = ar.al(32)
    topi = ar.al(32)
    vals8 = ar.al(8)
    if8 = ar.al(8)
    tvT = ar.al(128, parts=32)
    tcol = ar.al(8, parts=32)
    trow = ar.al(32, parts=1)
    gmrow = ar.al(8, parts=1)
    nmb = ar.al(8)
    exw = ar.al(32)
    wv = ar.al(32)
    rzb = ar.al(8)
    tmp1 = ar.al(8)
    tmp2 = ar.al(8)
    cesb = ar.al(128)
    for tc in range(4):
        for i in range(16):
            c.dma("sync", out=xcol[:, 512 * i:512 * i + 512],
                  in_=x3T[128 * i:128 * i + 128, 512 * tc:512 * tc + 512])
        for i in range(16):
            c.add("tensor", lambda e, l=gsb[:, 8 * i:8 * i + 8],
                  r=xcol[:, 512 * i:512 * i + 512], st=(i == 0), sp=(i == 15):
                  e.matmul(psA[0:8, 0:512], l, r, start=st, stop=sp))
        c.add("scalar", lambda e, d=lgT[:, 512 * tc:512 * tc + 512]:
              e.activation(d, psA[0:8, 0:512], AF.Copy, bias=0.0, scale=1.0))
    for tc in range(4):
        for q in range(4):
            c.add("tensor", lambda e, s=lgT[:, 512 * tc + 128 * q:512 * tc + 128 * q + 128],
                  ey=eyesb[0:8, 0:8]: e.transpose(psB[:, 0:8], s, ey))
            c.add("vector", lambda e, d=lg[:, 8 * (4 * tc + q):8 * (4 * tc + q) + 8]:
                  e.tensor_copy(d, psB[:, 0:8]))
    iu32 = ctx["idxu"]
    for tt in range(16):
        c.add("vector", lambda e, s=lg[:, 8 * tt:8 * tt + 8]:
              e.max_with_indices(vals8, iu32[:, 0:8], s))
        c.add("vector", lambda e: e.tensor_copy(if8, iu32[:, 0:8]))
        for k in range(2):
            c.add("vector", lambda e, d=topv[:, 16 * k + tt:16 * k + tt + 1], k=k:
                  e.tensor_copy(d, vals8[:, k:k + 1]))
            c.add("vector", lambda e, d=topi[:, 16 * k + tt:16 * k + tt + 1], k=k:
                  e.tensor_copy(d, if8[:, k:k + 1]))
    c.add("tensor", lambda e: e.transpose(psA[0:32, 0:128], topv, eyesb))
    c.add("vector", lambda e: e.tensor_copy(tvT, psA[0:32, 0:128]))
    c.add("vector", lambda e: e.tensor_reduce(tcol[:, 0:1], tvT, AX.X, ALU.max, False))
    c.add("tensor", lambda e: e.transpose(psB[0:1, 0:32], tcol[:, 0:1], eyesb[0:32, 0:32]))
    c.add("vector", lambda e: e.tensor_copy(trow, psB[0:1, 0:32]))
    c.add("vector", lambda e: e.tensor_reduce(gmrow[0:1, 0:1], trow[0:1, 0:16],
                                              AX.X, ALU.max, False))
    c.add("vector", lambda e: e.tensor_reduce(gmrow[0:1, 1:2], trow[0:1, 16:32],
                                              AX.X, ALU.max, False))
    c.dma("gpsimd", out=tm_i[0:1, 0:2], in_=gmrow[0:1, 0:2])
    c.coll("AllReduce", ALU.max, [[0, 1], [2, 3], [4, 5], [6, 7]], [tm_i], [tm_o])
    c.dma("sync", out=gmrow[0:1, 2:4], in_=tm_o[0:1, 0:2])
    c.add("vector", lambda e: e.tensor_scalar(gmrow[0:1, 4:6], gmrow[0:1, 2:4],
                                              -1.0, None, ALU.mult))
    c.add("tensor", lambda e: e.matmul(psB[:, 0:2], onesrF, gmrow[0:1, 4:6],
                                       start=True, stop=True))
    c.add("vector", lambda e: e.tensor_copy(nmb[:, 0:2], psB[:, 0:2]))
    for k in range(2):
        c.add("scalar", lambda e, k=k: e.activation(
            exw[:, 16 * k:16 * k + 16], topv[:, 16 * k:16 * k + 16], AF.Exp,
            bias=nmb[:, k:k + 1], scale=1.0))
    c.add("tensor", lambda e: e.transpose(psA[0:32, 128:256], exw, eyesb))
    c.add("vector", lambda e: e.tensor_copy(tvT, psA[0:32, 128:256]))
    c.add("vector", lambda e: e.tensor_reduce(tcol[:, 1:2], tvT, AX.X, ALU.add, False))
    c.add("tensor", lambda e: e.transpose(psB[0:1, 32:64], tcol[:, 1:2],
                                          eyesb[0:32, 0:32]))
    c.add("vector", lambda e: e.tensor_copy(trow, psB[0:1, 32:64]))
    c.add("vector", lambda e: e.tensor_reduce(gmrow[0:1, 6:7], trow[0:1, 0:16],
                                              AX.X, ALU.add, False))
    c.add("vector", lambda e: e.tensor_reduce(gmrow[0:1, 7:8], trow[0:1, 16:32],
                                              AX.X, ALU.add, False))
    c.dma("gpsimd", out=ts_i[0:1, 0:2], in_=gmrow[0:1, 6:8])
    c.coll("AllReduce", ALU.add, [[0, 1], [2, 3], [4, 5], [6, 7]], [ts_i], [ts_o])
    c.dma("sync", out=trow[0:1, 0:2], in_=ts_o[0:1, 0:2])
    c.add("vector", lambda e: e.reciprocal(trow[0:1, 2:4], trow[0:1, 0:2]))
    c.add("tensor", lambda e: e.matmul(psB[:, 2:4], onesrF, trow[0:1, 2:4],
                                       start=True, stop=True))
    c.add("vector", lambda e: e.tensor_copy(rzb[:, 0:2], psB[:, 2:4]))
    for k in range(2):
        c.add("vector", lambda e, k=k: e.tensor_scalar(
            wv[:, 16 * k:16 * k + 16], exw[:, 16 * k:16 * k + 16],
            rzb[:, k:k + 1], None, ALU.mult))
    for tt in range(16):
        for e8 in range(8):
            c.add("vector", lambda e, tt=tt, e8=e8: e.tensor_scalar(
                tmp1[:, 0:1], topi[:, tt:tt + 1], float(e8), None, ALU.is_equal))
            c.add("vector", lambda e, tt=tt: e.tensor_tensor(
                tmp1[:, 0:1], tmp1[:, 0:1], wv[:, tt:tt + 1], ALU.mult))
            c.add("vector", lambda e, tt=tt, e8=e8: e.tensor_scalar(
                tmp2[:, 0:1], topi[:, 16 + tt:16 + tt + 1], float(e8), None,
                ALU.is_equal))
            c.add("vector", lambda e, tt=tt, e8=e8: e.scalar_tensor_tensor(
                cesb[:, 16 * e8 + tt:16 * e8 + tt + 1], tmp2[:, 0:1],
                wv[:, 16 + tt:16 + tt + 1], tmp1[:, 0:1], ALU.mult, ALU.add))

    trow16 = arR.al(512, parts=16)
    asl = arR.al(256)
    for tc in range(4):
        for i in range(16):
            c.dma("sync", out=xcol[:, 512 * i:512 * i + 512],
                  in_=x3T[128 * i:128 * i + 128, 512 * tc:512 * tc + 512])
        for s in range(16):
            e8, ug = s // 2, s % 2
            c.dma("sync", out=asl, in_=(I["upAB"] if ug == 0 else I["gaAB"])[e8])
            for i in range(16):
                c.add("tensor", lambda e, l=asl[:, 16 * i:16 * i + 16],
                      r=xcol[:, 512 * i:512 * i + 512], st=(i == 0), sp=(i == 15):
                      e.matmul(psA[0:16, 0:512], l, r, start=st, stop=sp))
            c.add("scalar", lambda e: e.activation(trow16, psA[0:16, 0:512],
                                                   AF.Copy, bias=0.0, scale=2.0))
            c.dma("gpsimd", out=tD[s][:, 512 * tc:512 * tc + 512], in_=trow16)
    ar.pop(); arR.pop()

    ar.push(); arR.push()
    xcol2 = arR.al(16 * 1024)
    wup = arR.al(2048)
    wga = arR.al(2048)
    tup = arR.al(1024, parts=16)
    tga = arR.al(1024, parts=16)
    ubB = arR.al(1024, parts=16)
    gbB = arR.al(1024, parts=16)
    dnA = arR.al(128)
    hRR = arR.al(1024)
    chR = arR.al(1024)
    Tacc = ar.al(3 * 1024)
    TaccR = arR.al(1024, parts=16)
    bu = ar.al(1024)
    bg = ar.al(1024)
    uu = ar.al(1024)
    gg = ar.al(1024)
    su = ar.al(1024)
    mixf = ar.al(1024)
    cbc = ar.al(8 * 1024)
    c8t = ar.al(128, parts=8)
    crow = ar.al(1024, parts=1)
    for tb2 in range(2):
        for i in range(16):
            c.dma("sync", out=xcol2[:, 1024 * i:1024 * i + 1024],
                  in_=x3T[128 * i:128 * i + 128, 1024 * tb2:1024 * tb2 + 1024])
        for e8 in range(8):
            c.add("tensor", lambda e, s=cesb[0:128, 16 * e8 + 8 * tb2:16 * e8 + 8 * tb2 + 8]:
                  e.transpose(psB[0:8, 0:128], s, eyesb))
            c.add("vector", lambda e: e.tensor_copy(c8t, psB[0:8, 0:128]))
            for q in range(8):
                c.dma("sync", out=crow[0:1, 128 * q:128 * q + 128],
                      in_=c8t[q:q + 1, 0:128])
            for cc in range(2):
                c.add("tensor", lambda e, cc=cc: e.matmul(
                    psB[:, 512 * cc:512 * cc + 512], onesrF,
                    crow[0:1, 512 * cc:512 * cc + 512], start=True, stop=True))
            c.add("scalar", lambda e, d=cbc[:, 1024 * e8:1024 * e8 + 1024]:
                  e.activation(d, psB[:, 0:1024], AF.Copy, bias=0.0, scale=1.0))
        c.add("vector", lambda e: e.memset(Tacc, 0.0))
        for hc in range(44):
            c.dma("sync", out=wup, in_=I["WupTB"][hc])
            c.dma("sync", out=wga, in_=I["WgTB"][hc])
            c.dma("sync", out=ubB, in_=I["upBB"][hc])
            c.dma("sync", out=gbB, in_=I["gaBB"][hc])
            c.dma("sync", out=dnA, in_=I["dnAB"][hc])
            c.add("vector", lambda e: e.memset(mixf, 0.0))
            for half in range(2):
                for i in range(16):
                    c.add("tensor", lambda e, d=psA[:, 512 * half:512 * half + 512],
                          l=wup[:, 128 * i:128 * i + 128],
                          r=xcol2[:, 1024 * i + 512 * half:1024 * i + 512 * half + 512],
                          st=(i == 0), sp=(i == 15): e.matmul(d, l, r, start=st, stop=sp))
                for i in range(16):
                    c.add("tensor", lambda e, d=psA[:, 1024 + 512 * half:1536 + 512 * half],
                          l=wga[:, 128 * i:128 * i + 128],
                          r=xcol2[:, 1024 * i + 512 * half:1024 * i + 512 * half + 512],
                          st=(i == 0), sp=(i == 15): e.matmul(d, l, r, start=st, stop=sp))
            c.add("scalar", lambda e: e.activation(bu, psA[:, 0:1024], AF.Copy,
                                                   bias=0.0, scale=1.0))
            c.add("scalar", lambda e: e.activation(bg, psA[:, 1024:2048], AF.Copy,
                                                   bias=0.0, scale=1.0))
            for e8 in range(8):
                c.dma("sync", out=tup, in_=tD[2 * e8][:, 1024 * tb2:1024 * tb2 + 1024])
                c.dma("sync", out=tga, in_=tD[2 * e8 + 1][:, 1024 * tb2:1024 * tb2 + 1024])
                for half in range(2):
                    c.add("tensor", lambda e, half=half, e8=e8: e.matmul(
                        psB[:, 512 * half:512 * half + 512],
                        ubB[:, 128 * e8:128 * e8 + 128],
                        tup[:, 512 * half:512 * half + 512], start=True, stop=True))
                c.add("vector", lambda e: e.tensor_tensor(uu, psB[:, 0:1024], bu,
                                                          ALU.add))
                c.add("scalar", lambda e: e.activation(su, uu, AF.Silu, bias=0.0,
                                                       scale=1.0))
                for half in range(2):
                    c.add("tensor", lambda e, half=half, e8=e8: e.matmul(
                        psB[:, 1024 + 512 * half:1536 + 512 * half],
                        gbB[:, 128 * e8:128 * e8 + 128],
                        tga[:, 512 * half:512 * half + 512], start=True, stop=True))
                c.add("vector", lambda e: e.tensor_tensor(gg, psB[:, 1024:2048], bg,
                                                          ALU.add))
                c.add("vector", lambda e: e.tensor_tensor(hRR, su, gg, ALU.mult))
                c.add("vector", lambda e, e8=e8: e.tensor_tensor(
                    chR, hRR.bitcast(F32), cbc[:, 1024 * e8:1024 * e8 + 1024],
                    ALU.mult))
                for half in range(2):
                    c.add("tensor", lambda e, half=half, e8=e8: e.matmul(
                        psB[0:16, 512 * half:512 * half + 512],
                        dnA[:, 16 * e8:16 * e8 + 16],
                        chR[:, 512 * half:512 * half + 512], start=True, stop=True))
                tsl = Tacc[32 * (e8 % 3):32 * (e8 % 3) + 16,
                           1024 * (e8 // 3):1024 * (e8 // 3) + 1024] if False else None
                base = 32 * (e8 % 3)
                cr = 1024 * (e8 // 3)
                c.add("vector", lambda e, d=Tacc[base:base + 16, cr:cr + 1024]:
                      e.tensor_tensor(d, d, psB[0:16, 0:1024], ALU.add))
                c.add("vector", lambda e: e.tensor_tensor(mixf, mixf,
                                                          chR.bitcast(F32), ALU.add))
            c.dma("gpsimd", out=mixD[128 * hc:128 * hc + 128,
                                     1024 * tb2:1024 * tb2 + 1024].bitcast(F32),
                  in_=mixf)
        for e8 in range(8):
            base = 32 * (e8 % 3)
            cr = 1024 * (e8 // 3)
            c.add("vector", lambda e, d=TaccR, s=Tacc[base:base + 16, cr:cr + 1024]:
                  e.tensor_scalar(d, s, 2.0, None, ALU.mult))
            c.dma("gpsimd", out=TaD[e8][:, 1024 * tb2:1024 * tb2 + 1024], in_=TaccR)
    ar.pop(); arR.pop()

    ar.push(); arR.push()
    mrhs = arR.al(1024)
    wdsl = arR.al(4 * 128)
    tasl = arR.al(8 * 1024, parts=16)
    dnB2 = arR.al(128, parts=16)
    oet4 = ar.al(4 * 1024)
    oev2 = ar.al(512)
    for tb2 in range(2):
        for e8 in range(8):
            c.dma("sync", out=tasl[:, 1024 * e8:1024 * e8 + 1024],
                  in_=TaD[e8][:, 1024 * tb2:1024 * tb2 + 1024])
        for eg in range(4):
            for hc in range(44):
                c.dma("sync", out=mrhs, in_=mixD[128 * hc:128 * hc + 128,
                                                 1024 * tb2:1024 * tb2 + 1024])
                for i in range(4):
                    ec = 4 * eg + i
                    c.dma("sync", out=wdsl[:, 128 * i:128 * i + 128],
                          in_=I["WdTB"][ec][:, 128 * hc:128 * hc + 128])
                for i in range(4):
                    for half in range(2):
                        c.add("tensor", lambda e, i=i, half=half, hc=hc: e.matmul(
                            psA[:, 0:512] if False else
                            (psA if i < 2 else psB)[:, 1024 * (i % 2) + 512 * half:
                                                    1024 * (i % 2) + 512 * half + 512],
                            wdsl[:, 128 * i:128 * i + 128],
                            mrhs[:, 512 * half:512 * half + 512],
                            start=(hc == 0), stop=False))
            for i in range(4):
                ec = 4 * eg + i
                for e8 in range(8):
                    c.dma("sync", out=dnB2, in_=I["dnBT"][e8][:, 128 * ec:128 * ec + 128])
                    for half in range(2):
                        c.add("tensor", lambda e, i=i, half=half, e8=e8, sp=(e8 == 7):
                              e.matmul((psA if i < 2 else psB)
                                       [:, 1024 * (i % 2) + 512 * half:
                                        1024 * (i % 2) + 512 * half + 512],
                                       dnB2,
                                       tasl[:, 1024 * e8 + 512 * half:
                                            1024 * e8 + 512 * half + 512],
                                       start=False, stop=sp))
            for i in range(4):
                pst = psA if i < 2 else psB
                c.add("scalar", lambda e, d=oet4[:, 1024 * i:1024 * i + 1024],
                      s=pst[:, 1024 * (i % 2):1024 * (i % 2) + 1024]:
                      e.activation(d, s, AF.Copy, bias=0.0, scale=1.0))
            for i in range(4):
                ec = 4 * eg + i
                for q in range(8):
                    pst2 = psA if q % 8 < 4 else psB
                    c.add("tensor", lambda e, q=q, pst2=pst2,
                          s=oet4[:, 1024 * i + 128 * q:1024 * i + 128 * q + 128],
                          ey=eyesb:
                          e.transpose(pst2[:, 128 * (q % 4):128 * (q % 4) + 128], s, ey))
                    if q % 4 == 3:
                        c.add("scalar", lambda e, pst2=pst2, d=oev2:
                              e.activation(d, pst2[:, 0:512], AF.Copy,
                                           bias=0.0, scale=1.0))
                        for w in range(4):
                            tq = q - 3 + w
                            c.dma("gpsimd",
                                  out=o2D[1024 * tb2 + 128 * tq:1024 * tb2 + 128 * tq + 128,
                                          128 * ec:128 * ec + 128],
                                  in_=oev2[:, 128 * w:128 * w + 128])
    ar.pop(); arR.pop()
    ln_pass(ctx, o2D, x3D, out_own)


def kernel(**ins):
    maps = [host_prep(ins, core) for core in range(8)]
    nc = build(stages=("A", "B", "C", "D"))
    res = run_bass_kernel_spmd(nc, maps, core_ids=list(range(8)))
    out = np.zeros((B, T, E), np.float32)
    for core in range(8):
        b, par = core // 2, core % 2
        oo = np.asarray(res.results[core]["out_own"])
        for j in range(NOT_):
            out[b, 128 * (2 * j + par):128 * (2 * j + par) + 128] = oo[128 * j:128 * j + 128]
    return out
